# version 02
# baseline (speedup 1.0000x reference)
import math
from contextlib import ExitStack

import numpy as np
import ml_dtypes

import concourse.bass as bass
import concourse.mybir as mybir
from concourse.bass_utils import run_bass_kernel_spmd

F32 = mybir.dt.float32
BF16 = mybir.dt.bfloat16
I32 = mybir.dt.int32
AF = mybir.ActivationFunctionType
ALU = mybir.AluOpType
BF = ml_dtypes.bfloat16

NC = 8
D = 2048
T = 8192
TC = T // NC
NCH = D // 128
INTER = 5504
NOT = INTER // 128
EPS = 1e-5
NEG = -1.0e30

_prog_cache = {}


def cached(name):
    def deco(fn):
        def get():
            if name not in _prog_cache:
                _prog_cache[name] = fn()
            return _prog_cache[name]
        return get
    return deco


class Prog:
    def __init__(self, nc, es):
        self.nc = nc
        self.es = es
        self._sems = {}
        self.cnt = {}

    def sem(self, name):
        if name not in self._sems:
            self._sems[name] = self.es.enter_context(self.nc.semaphore(name))
            self.cnt[name] = 0
        return self._sems[name]

    def inc(self, instr, name, n=1):
        s = self.sem(name)
        self.cnt[name] += n
        instr.then_inc(s, n)
        return instr

    def wait(self, eng, name, val=None):
        eng.wait_ge(self.sem(name), self.cnt[name] if val is None else val)

    def chain(self, eng, ins, name="vch"):
        self.inc(ins, name, 1)
        eng.wait_ge(self.sem(name), self.cnt[name])
        return ins



def emit_ln(p, blk, tag, nch, tf, emit_x_loads, xbuf,
            s_sq, s_xb, s_sqb, s_mb, s_rb, s_onesM, ps_sx, ps_sq):
    emit_x_loads(0)

    @blk.scalar
    def _(a):
        for c in range(nch):
            a.wait_ge(p.sem(f"x{tag}0{c % 2}"), 16 * (c // 2 + 1))
            a.wait_ge(p.sem(f"sqb{tag}"), c)
            p.inc(a.activation(s_sq[:, 0:tf], xbuf[c % 2][:, 0:tf], AF.Square),
                  f"sq{tag}", 1)

    @blk.vector
    def _(v):
        for c in range(nch):
            v.wait_ge(p.sem(f"smm{tag}"), c)
            v.wait_ge(p.sem(f"x{tag}0{c % 2}"), 16 * (c // 2 + 1))
            p.inc(v.tensor_copy(s_xb[:, 0:tf], xbuf[c % 2][:, 0:tf]),
                  f"xu{tag}0", 1)
            v.wait_ge(p.sem(f"sq{tag}"), c + 1)
            p.inc(v.tensor_copy(s_sqb[:, 0:tf], s_sq[:, 0:tf]), f"sqb{tag}", 1)

    @blk.tensor
    def _(t):
        p.wait(t, "consts")
        for c in range(nch):
            t.wait_ge(p.sem(f"sqb{tag}"), c + 1)
            for f0 in range(0, tf, 512):
                f1 = min(f0 + 512, tf)
                t.matmul(ps_sx[:, f0:f1], s_onesM[:, 0:128], s_xb[:, f0:f1],
                         start=(c == 0), stop=(c == nch - 1))
                ins = t.matmul(ps_sq[:, f0:f1], s_onesM[:, 0:128],
                               s_sqb[:, f0:f1],
                               start=(c == 0), stop=(c == nch - 1))
            p.inc(ins, f"smm{tag}", 1)

    @blk.vector
    def _(v):
        v.wait_ge(p.sem(f"smm{tag}"), nch)
        n = float(nch * 128)
        p.chain(v, v.tensor_scalar(s_mb[:, 0:tf], ps_sx[:, 0:tf], 1.0 / n,
                                   None, ALU.mult))
        p.chain(v, v.tensor_scalar(s_rb[:, 0:tf], ps_sq[:, 0:tf], 1.0 / n,
                                   None, ALU.mult))
        p.chain(v, v.tensor_tensor(s_sq[:, 0:tf], s_mb[:, 0:tf], s_mb[:, 0:tf],
                                   ALU.mult))
        p.chain(v, v.tensor_tensor(s_rb[:, 0:tf], s_rb[:, 0:tf], s_sq[:, 0:tf],
                                   ALU.subtract))
        p.inc(v.tensor_scalar(s_rb[:, 0:tf], s_rb[:, 0:tf], EPS, None,
                              ALU.add), f"var{tag}", 1)

    @blk.scalar
    def _(a):
        p.wait(a, f"var{tag}")
        p.inc(a.activation(s_rb[:, 0:tf], s_rb[:, 0:tf], AF.Ln), f"lnv{tag}", 1)
        a.wait_ge(p.sem(f"lnv{tag}"), p.cnt[f"lnv{tag}"])
        p.inc(a.activation(s_rb[:, 0:tf], s_rb[:, 0:tf], AF.Exp, scale=-0.5),
              f"rstd{tag}", 1)



GSH = INTER // NC
DSH = D // NC


@cached("decode")
def prog_decode():
    nc = bass.Bass("TRN2", target_bir_lowering=False, debug=False)
    gi = nc.dram_tensor("gi", [GSH, D], I32, kind="ExternalInput").ap()
    ui = nc.dram_tensor("ui", [GSH, D], I32, kind="ExternalInput").ap()
    di = nc.dram_tensor("di", [DSH, INTER], I32, kind="ExternalInput").ap()
    lut = nc.dram_tensor("lut", [1, 16], F32, kind="ExternalInput").ap()
    wg = nc.dram_tensor("wg", [GSH, D], BF16, kind="ExternalOutput").ap()
    wu = nc.dram_tensor("wu", [GSH, D], BF16, kind="ExternalOutput").ap()
    wd = nc.dram_tensor("wd", [DSH, INTER], BF16, kind="ExternalOutput").ap()

    with ExitStack() as es:
        e = es.enter_context
        p = Prog(nc, es)
        s_lut = e(nc.sbuf_tensor("s_lut", [1, 16], F32))
        s_lutb = e(nc.sbuf_tensor("s_lutb", [1, 16], BF16))
        s_ones = e(nc.sbuf_tensor("s_ones", [1, 128], BF16))
        s_lr = e(nc.sbuf_tensor("s_lr", [128, 16], F32))
        s_lrh = e(nc.sbuf_tensor("s_lrh", [128, 16], F32))
        s_i = [e(nc.sbuf_tensor(f"s_i{b}", [128, INTER], I32)) for b in range(2)]
        s_f = e(nc.sbuf_tensor("s_f", [128, INTER], BF16))
        s_v = e(nc.sbuf_tensor("s_v", [128, INTER], BF16))
        s_a = [e(nc.sbuf_tensor(f"s_a{b}", [128, INTER], BF16)) for b in range(2)]
        ps_l = e(nc.psum_tensor("ps_l", [128, 512], F32))
        blk = e(nc.Block())

        jobs = []
        for r0 in range(0, GSH, 128):
            jobs.append((gi, wg, r0, min(128, GSH - r0), D, 0))
        for r0 in range(0, GSH, 128):
            jobs.append((ui, wu, r0, min(128, GSH - r0), D, 0))
        for r0 in range(0, DSH, 128):
            jobs.append((di, wd, r0, min(128, DSH - r0), INTER, 1))
        nj = len(jobs)

        @blk.sync
        def _(sync):
            p.inc(sync.dma_start(out=s_lut[:], in_=lut), "ldlut", 16)
            for j, (iap, oap, r0, rows, cols, half) in enumerate(jobs):
                if j >= 2:
                    sync.wait_ge(p.sem("dec"), j - 1)
                p.inc(sync.dma_start(out=s_i[j % 2][0:rows, 0:cols],
                                     in_=iap[r0:r0 + rows, :]), f"ld{j % 2}", 16)

        @blk.vector
        def _(v):
            v.memset(s_ones[:], 1.0)
            v.wait_ge(p.sem("ldlut"), 16)
            p.inc(v.tensor_copy(s_lutb[:], s_lut[:]), "lb", 1)

        @blk.tensor
        def _(t):
            p.wait(t, "lb")
            p.inc(t.matmul(ps_l[:, 0:16], s_ones[:], s_lutb[:], start=True,
                           stop=True), "lutmm", 1)

        @blk.vector
        def _(v):
            p.wait(v, "lutmm")
            p.chain(v, v.tensor_copy(s_lr[:], ps_l[:, 0:16]))
            p.chain(v, v.tensor_scalar(s_lrh[:], s_lr[:], 0.5, None, ALU.mult))
            for j, (iap, oap, r0, rows, cols, half) in enumerate(jobs):
                b = j % 2
                v.wait_ge(p.sem(f"ld{b}"), 16 * (j // 2 + 1))
                if j >= 2:
                    v.wait_ge(p.sem(f"st{b}"), 16 * (j // 2))
                lr = s_lrh if half else s_lr
                csl = [(k, min(k + 2048, cols)) for k in range(0, cols, 2048)]
                for c0, c1 in csl:
                    v.tensor_copy(s_f[0:rows, c0:c1], s_i[b][0:rows, c0:c1])
                for jj in range(16):
                    for si, (c0, c1) in enumerate(csl):
                        v.tensor_scalar(s_v[0:rows, c0:c1], s_f[0:rows, c0:c1],
                                        float(jj), lr[0:rows, jj:jj + 1],
                                        ALU.is_equal, ALU.mult)
                        if jj == 0:
                            v.tensor_copy(s_a[b][0:rows, c0:c1],
                                          s_v[0:rows, c0:c1])
                        else:
                            ins = v.tensor_tensor(s_a[b][0:rows, c0:c1],
                                                  s_a[b][0:rows, c0:c1],
                                                  s_v[0:rows, c0:c1], ALU.add)
                            if jj == 15 and si == len(csl) - 1:
                                p.inc(ins, "dec", 1)

        @blk.gpsimd
        def _(g):
            for j, (iap, oap, r0, rows, cols, half) in enumerate(jobs):
                g.wait_ge(p.sem("dec"), j + 1)
                p.inc(g.dma_start(out=oap[r0:r0 + rows, :],
                                  in_=s_a[j % 2][0:rows, 0:cols]), f"st{j % 2}", 16)
            g.wait_ge(p.sem("st0"), 16 * ((nj + 1) // 2))
            g.wait_ge(p.sem("st1"), 16 * (nj // 2))

    return nc



@cached("ffn")
def prog_ffn():
    import os
    STAGE = os.environ.get("FFN_STAGE", "full")
    nc = bass.Bass("TRN2", target_bir_lowering=False, debug=False)
    xT = nc.dram_tensor("xT", [D, TC], F32, kind="ExternalInput").ap()
    wgT = nc.dram_tensor("wgT", [D, INTER], BF16, kind="ExternalInput").ap()
    wuT = nc.dram_tensor("wuT", [D, INTER], BF16, kind="ExternalInput").ap()
    wdT = nc.dram_tensor("wdT", [INTER, D], BF16, kind="ExternalInput").ap()
    yT = nc.dram_tensor("yT", [D, TC], F32, kind="ExternalOutput").ap()
    TF = TC
    NTT = TF // 128

    with ExitStack() as es:
        e = es.enter_context
        p = Prog(nc, es)
        s_x = [e(nc.sbuf_tensor(f"s_x{b}", [128, TF], F32)) for b in range(2)]
        s_xln = e(nc.sbuf_tensor("s_xln", [128, NCH * TF], BF16))
        s_hM = e(nc.sbuf_tensor("s_hM", [128, NOT * TF], BF16))
        s_sq = e(nc.sbuf_tensor("s_sq", [128, TF], F32))
        s_xb = e(nc.sbuf_tensor("s_xb", [128, TF], BF16))
        s_sqb = e(nc.sbuf_tensor("s_sqb", [128, TF], BF16))
        s_mb = e(nc.sbuf_tensor("s_mb", [128, TF], F32))
        s_rb = e(nc.sbuf_tensor("s_rb", [128, TF], F32))
        s_sil = e(nc.sbuf_tensor("s_sil", [128, TF], BF16))
        s_u = e(nc.sbuf_tensor("s_u", [128, TF], BF16))
        s_onesM = e(nc.sbuf_tensor("s_onesM", [128, 128], BF16))
        s_wg = [e(nc.sbuf_tensor(f"s_wg{b}", [128, 128], BF16)) for b in range(2)]
        s_wu = [e(nc.sbuf_tensor(f"s_wu{b}", [128, 128], BF16)) for b in range(2)]
        s_wd = [e(nc.sbuf_tensor(f"s_wd{b}", [128, 256], BF16)) for b in range(2)]
        ps_g = e(nc.psum_tensor("ps_g0", [128, TF], F32))
        ps_u = e(nc.psum_tensor("ps_u0", [128, TF], F32))
        blk = e(nc.Block())

        @blk.vector
        def _(v):
            p.inc(v.memset(s_onesM[:], 1.0), "consts", 1)

        def loads_x(pid):
            @blk.sync
            def _(sync):
                for c in range(NCH):
                    if c >= 2:
                        sync.wait_ge(p.sem(f"xuffn{pid}"), c - 1)
                        if pid == 0:
                            sync.wait_ge(p.sem("sqffn"), c - 1)
                    elif pid == 1:
                        sync.wait_ge(p.sem("sqffn"), NCH)
                        sync.wait_ge(p.sem("xuffn0"), NCH)
                    p.inc(sync.dma_start(out=s_x[c % 2][:],
                                         in_=xT[c * 128:(c + 1) * 128, :]),
                          f"xffn{pid}{c % 2}", 16)

        emit_ln(p, blk, "ffn", NCH, TF, loads_x, s_x,
                s_sq, s_xb, s_sqb, s_mb, s_rb, s_onesM, ps_g, ps_u)

        if STAGE == "lnvar":
            @blk.vector
            def _(v):
                v.wait_ge(p.sem("varffn"), 1)
                p.inc(v.tensor_copy(s_x[0][:], s_mb[:]), "dbg", 1)
                p.inc(v.tensor_copy(s_x[1][:], s_rb[:]), "dbg", 1)

            @blk.gpsimd
            def _(g):
                g.wait_ge(p.sem("dbg"), 2)
                p.inc(g.dma_start(out=yT[0:128, :], in_=s_x[0][:]), "sty0", 16)
                p.inc(g.dma_start(out=yT[128:256, :], in_=s_x[1][:]), "sty1", 16)
                g.wait_ge(p.sem("sty0"), 16)
                g.wait_ge(p.sem("sty1"), 16)
            return nc

        if STAGE == "ln":
            @blk.vector
            def _(v):
                v.wait_ge(p.sem("rstdffn"), 1)
                p.inc(v.tensor_copy(s_x[0][:], s_mb[:]), "dbg", 1)
                p.inc(v.tensor_copy(s_x[1][:], s_rb[:]), "dbg", 1)

            @blk.gpsimd
            def _(g):
                g.wait_ge(p.sem("dbg"), 2)
                p.inc(g.dma_start(out=yT[0:128, :], in_=s_x[0][:]), "sty", 16)
                p.inc(g.dma_start(out=yT[128:256, :], in_=s_x[1][:]), "sty", 16)
                g.wait_ge(p.sem("sty"), 32)
            return nc

        loads_x(1)

        @blk.vector
        def _(v):
            v.wait_ge(p.sem("rstdffn"), 1)
            for c in range(NCH):
                v.wait_ge(p.sem(f"xffn1{c % 2}"), 16 * (c // 2 + 1))
                p.chain(v, v.tensor_tensor(s_sq[:], s_x[c % 2][:], s_mb[:],
                                           ALU.subtract))
                ins = v.tensor_tensor(s_xln[:, c * TF:(c + 1) * TF], s_sq[:],
                                      s_rb[:], ALU.mult)
                p.inc(ins, "xuffn1", 1)
                p.wait(v, "xuffn1")

        if STAGE == "apply":
            @blk.gpsimd
            def _(g):
                for c in range(NCH):
                    g.wait_ge(p.sem("xuffn1"), c + 1)
                    p.inc(g.dma_start(out=yT[c * 128:(c + 1) * 128, :].bitcast(BF16)[:, 0:TC],
                                      in_=s_xln[:, c * TF:(c + 1) * TF]), f"sty{c % 2}", 16)
                g.wait_ge(p.sem("sty0"), 16 * (NCH // 2))
                g.wait_ge(p.sem("sty1"), 16 * (NCH // 2))
            return nc

        @blk.sync
        def _(sync):
            for g in range(NOT):
                for c in range(NCH):
                    j = g * NCH + c
                    if j >= 2:
                        sync.wait_ge(p.sem("wcons"), j - 1)
                    p.inc(sync.dma_start(out=s_wg[j % 2][:, 0:128],
                                         in_=wgT[c * 128:(c + 1) * 128,
                                                 g * 128:(g + 1) * 128]),
                          f"ldw{j % 2}", 16)
                    p.inc(sync.dma_start(out=s_wu[j % 2][:, 0:128],
                                         in_=wuT[c * 128:(c + 1) * 128,
                                                 g * 128:(g + 1) * 128]),
                          f"ldw{j % 2}", 16)

        @blk.tensor
        def _(t):
            t.wait_ge(p.sem("varffn"), 1)
            t.wait_ge(p.sem("xuffn1"), NCH)
            for g in range(NOT):
                if g > 0:
                    t.wait_ge(p.sem("guev"), g)
                    t.wait_ge(p.sem("gsil"), g)
                for c in range(NCH):
                    j = g * NCH + c
                    t.wait_ge(p.sem(f"ldw{j % 2}"), 32 * (j // 2 + 1))
                    for hh in range(2):
                        f0, f1 = hh * 512, (hh + 1) * 512
                        t.matmul(ps_g[:, f0:f1], s_wg[j % 2][:, 0:128],
                                 s_xln[:, c * TF + f0:c * TF + f1],
                                 start=(c == 0), stop=(c == NCH - 1))
                        ins = t.matmul(ps_u[:, f0:f1], s_wu[j % 2][:, 0:128],
                                       s_xln[:, c * TF + f0:c * TF + f1],
                                       start=(c == 0), stop=(c == NCH - 1))
                    p.inc(ins, "wcons", 1)

        @blk.scalar
        def _(a):
            for g in range(NOT):
                a.wait_ge(p.sem("wcons"), (g + 1) * NCH)
                if g > 0:
                    a.wait_ge(p.sem("ghm"), g)
                p.inc(a.activation(s_sil[:], ps_g[:, 0:TF], AF.Silu), "gsil", 1)

        @blk.vector
        def _(v):
            for g in range(NOT):
                v.wait_ge(p.sem("gsil"), g + 1)
                p.inc(v.tensor_copy(s_u[:], ps_u[:, 0:TF]), "guev", 1)
                p.wait(v, "guev")
                ins = v.tensor_tensor(s_hM[:, g * TF:(g + 1) * TF],
                                      s_sil[:], s_u[:], ALU.mult)
                p.inc(ins, "ghm", 1)

        if STAGE == "gu":
            @blk.gpsimd
            def _(g):
                g.wait_ge(p.sem("ghm"), NOT)
                for c in range(4):
                    p.inc(g.dma_start(out=yT[c * 128:(c + 1) * 128, :].bitcast(BF16)[:, 0:TC],
                                      in_=s_hM[:, c * TF:(c + 1) * TF]), f"sty{c % 2}", 16)
                g.wait_ge(p.sem("sty0"), 32)
                g.wait_ge(p.sem("sty1"), 32)
            return nc

        @blk.sync
        def _(sync):
            for hp in range(8):
                for ot in range(NOT):
                    j = hp * NOT + ot
                    if j >= 2:
                        sync.wait_ge(p.sem("dcons"), j - 1)
                    p.inc(sync.dma_start(out=s_wd[j % 2][:, 0:256],
                                         in_=wdT[ot * 128:(ot + 1) * 128,
                                                 hp * 256:(hp + 1) * 256]),
                          f"ldd{j % 2}", 16)

        @blk.tensor
        def _(t):
            t.wait_ge(p.sem("ghm"), NOT)
            for hp in range(8):
                if hp > 0:
                    t.wait_ge(p.sem("yrdy"), 4 * hp)
                for ot in range(NOT):
                    j = hp * NOT + ot
                    t.wait_ge(p.sem(f"ldd{j % 2}"), 16 * (j // 2 + 1))
                    for hc in range(2):
                        pg = (ps_g, ps_u)[hc]
                        t.matmul(pg[:, 0:512],
                                 s_wd[j % 2][:, hc * 128:(hc + 1) * 128],
                                 s_hM[:, ot * TF:ot * TF + 512],
                                 start=(ot == 0), stop=(ot == NOT - 1))
                        ins = t.matmul(pg[:, 512:1024],
                                       s_wd[j % 2][:, hc * 128:(hc + 1) * 128],
                                       s_hM[:, ot * TF + 512:ot * TF + 1024],
                                       start=(ot == 0), stop=(ot == NOT - 1))
                    p.inc(ins, "dcons", 1)

        @blk.vector
        def _(v):
            for hp in range(8):
                v.wait_ge(p.sem("dcons"), (hp + 1) * NOT)
                for hc in range(2):
                    hid = hp * 2 + hc
                    pg = (ps_g, ps_u)[hc]
                    v.wait_ge(p.sem(f"ldxr{hid % 2}"), 16 * (hid // 2 + 1))
                    p.inc(v.tensor_tensor(s_x[hid % 2][:, 0:512], pg[:, 0:512],
                                          s_x[hid % 2][:, 0:512], ALU.add),
                          "yrdy", 1)
                    p.inc(v.tensor_tensor(s_x[hid % 2][:, 512:1024],
                                          pg[:, 512:1024],
                                          s_x[hid % 2][:, 512:1024], ALU.add),
                          "yrdy", 1)

        @blk.gpsimd
        def _(g):
            for hid in range(NCH):
                g.wait_ge(p.sem("dcons"), (hid // 2 + 1) * NOT)
                if hid >= 2:
                    g.wait_ge(p.sem(f"sty{hid % 2}"), 16 * (hid // 2))
                p.inc(g.dma_start(out=s_x[hid % 2][:],
                                  in_=xT[hid * 128:(hid + 1) * 128, :]),
                      f"ldxr{hid % 2}", 16)
                g.wait_ge(p.sem("yrdy"), 2 * (hid + 1))
                p.inc(g.dma_start(out=yT[hid * 128:(hid + 1) * 128, :],
                                  in_=s_x[hid % 2][:]), f"sty{hid % 2}", 16)
            g.wait_ge(p.sem("sty0"), 16 * (NCH // 2))
            g.wait_ge(p.sem("sty1"), 16 * (NCH // 2))

    return nc



IDENT = np.eye(128, dtype=np.float32).astype(BF)


def _np(a, dt=np.float32):
    return np.ascontiguousarray(np.asarray(a, dtype=dt))


def _run(prog, ims):
    return run_bass_kernel_spmd(prog, ims, core_ids=list(range(NC))).results


@cached("lnx")
def prog_lnx():
    nc = bass.Bass("TRN2", target_bir_lowering=False, debug=False)
    xT = nc.dram_tensor("xT", [D, TC], F32, kind="ExternalInput").ap()
    lnT = nc.dram_tensor("lnT", [D, TC], F32, kind="ExternalOutput").ap()

    with ExitStack() as es:
        e = es.enter_context
        p = Prog(nc, es)
        s_x = [e(nc.sbuf_tensor(f"s_x{b}", [128, TC], F32)) for b in range(2)]
        s_o = [e(nc.sbuf_tensor(f"s_o{b}", [128, TC], F32)) for b in range(2)]
        s_sq = e(nc.sbuf_tensor("s_sq", [128, TC], F32))
        s_xb = e(nc.sbuf_tensor("s_xb", [128, TC], BF16))
        s_sqb = e(nc.sbuf_tensor("s_sqb", [128, TC], BF16))
        s_mb = e(nc.sbuf_tensor("s_mb", [128, TC], F32))
        s_rb = e(nc.sbuf_tensor("s_rb", [128, TC], F32))
        s_onesM = e(nc.sbuf_tensor("s_onesM", [128, 128], BF16))
        ps_sx = e(nc.psum_tensor("ps_sx", [128, TC], F32))
        ps_sq = e(nc.psum_tensor("ps_sq", [128, TC], F32))
        blk = e(nc.Block())

        @blk.vector
        def _(v):
            p.inc(v.memset(s_onesM[:], 1.0), "consts", 1)

        def loads_x(pid):
            @blk.sync
            def _(sync):
                for c in range(NCH):
                    if c >= 2:
                        sync.wait_ge(p.sem("lnapp" if pid else "xulnx0"),
                                     c - 1)
                        if pid == 0:
                            sync.wait_ge(p.sem("sqlnx"), c - 1)
                    elif pid == 1:
                        sync.wait_ge(p.sem("rstdlnx"), 1)
                    p.inc(sync.dma_start(out=s_x[c % 2][:],
                                         in_=xT[c * 128:(c + 1) * 128, :]),
                          f"xlnx{pid}{c % 2}", 16)

        emit_ln(p, blk, "lnx", NCH, TC, loads_x, s_x,
                s_sq, s_xb, s_sqb, s_mb, s_rb, s_onesM, ps_sx, ps_sq)

        loads_x(1)

        @blk.vector
        def _(v):
            v.wait_ge(p.sem("rstdlnx"), 1)
            for c in range(NCH):
                v.wait_ge(p.sem(f"xlnx1{c % 2}"), 16 * (c // 2 + 1))
                if c >= 2:
                    v.wait_ge(p.sem(f"sto{c % 2}"), 16 * (c // 2))
                p.chain(v, v.tensor_tensor(s_sq[:], s_x[c % 2][:], s_mb[:],
                                           ALU.subtract))
                p.inc(v.tensor_tensor(s_o[c % 2][:], s_sq[:], s_rb[:],
                                      ALU.mult), "lnapp", 1)
                p.wait(v, "lnapp")

        @blk.gpsimd
        def _(g):
            for c in range(NCH):
                g.wait_ge(p.sem("lnapp"), c + 1)
                p.inc(g.dma_start(out=lnT[c * 128:(c + 1) * 128, :],
                                  in_=s_o[c % 2][:]), f"sto{c % 2}", 16)
            g.wait_ge(p.sem("sto0"), 16 * 8)
            g.wait_ge(p.sem("sto1"), 16 * 8)

    return nc


def _col(a):
    return np.ascontiguousarray(np.asarray(a, np.float32).reshape(NCH, 128).T)


def kernel(**inp):
    x = _np(inp["L0_attn_x"]).reshape(T, D)
    xT = np.ascontiguousarray(x.T)

    lut = _np(inp["L0_ffn_lut"]).reshape(1, 16)
    gi = np.ascontiguousarray(np.asarray(inp["L0_ffn_gate_idx"], dtype=np.int32))
    ui = np.ascontiguousarray(np.asarray(inp["L0_ffn_up_idx"], dtype=np.int32))
    di = np.ascontiguousarray(np.asarray(inp["L0_ffn_down_idx"], dtype=np.int32))
    r = _run(prog_decode(), [dict(gi=gi[c * GSH:(c + 1) * GSH],
                                  ui=ui[c * GSH:(c + 1) * GSH],
                                  di=di[c * DSH:(c + 1) * DSH], lut=lut)
                             for c in range(NC)])
    wgT = np.ascontiguousarray(np.concatenate(
        [np.asarray(rr["wg"]) for rr in r], axis=0).T)
    wuT = np.ascontiguousarray(np.concatenate(
        [np.asarray(rr["wu"]) for rr in r], axis=0).T)
    wdT = np.ascontiguousarray(np.concatenate(
        [np.asarray(rr["wd"]) for rr in r], axis=0).T)

    def run_ffn(xTf):
        r = _run(prog_ffn(), [dict(xT=np.ascontiguousarray(xTf[:, c * TC:(c + 1) * TC]),
                                   wgT=wgT, wuT=wuT, wdT=wdT)
                              for c in range(NC)])
        return np.concatenate([np.asarray(rr["yT"], np.float32) for rr in r], axis=1)

    x1T = run_ffn(xT)

    Wq = _np(inp["L0_attn_Wq"])
    bq = _np(inp["L0_attn_bq"]).reshape(D)
    Wkv = _np(inp["L0_attn_Wkv"])
    bkv = _np(inp["L0_attn_bkv"]).reshape(1024)
    Wo0 = _np(inp["L0_attn_Wo"])
    bo = _np(inp["L0_attn_bo"]).reshape(D)
    x1 = np.ascontiguousarray(x1T.T)
    r = _run(prog_attn(), [attn_host_inputs(c, x1, Wq, bq, Wkv, bkv, Wo0, bo)
                           for c in range(NC)])
    x2T = np.concatenate([np.asarray(rr["yT"], np.float32) for rr in r],
                         axis=1)

    x3T = run_ffn(x2T)

    WgM = _np(inp["L1_ffn_Wg"])
    W1 = _np(inp["L1_ffn_W1"])
    W3 = _np(inp["L1_ffn_W3"])
    W2 = _np(inp["L1_ffn_W2"])
    biasM = _np(inp["L1_ffn_bias"]).reshape(1, NE)
    moe_w = dict(wgT=np.ascontiguousarray(WgM.T),
                 w1T=np.ascontiguousarray(np.transpose(W1, (0, 2, 1))),
                 w3T=np.ascontiguousarray(np.transpose(W3, (0, 2, 1))),
                 w2T=np.ascontiguousarray(np.transpose(W2, (0, 2, 1))),
                 bias=biasM, sel8=SEL8, ident=IDENT)

    def run_moe(xTf):
        r = _run(prog_moe(), [dict(xT=np.ascontiguousarray(xTf[:, c * TC:(c + 1) * TC]),
                                   **moe_w) for c in range(NC)])
        return np.concatenate([np.asarray(rr["yT"], np.float32) for rr in r], axis=1)

    x4T = run_moe(x3T)

    r = _run(prog_lnx(), [dict(xT=np.ascontiguousarray(x4T[:, c * TC:(c + 1) * TC]))
                          for c in range(NC)])
    ln4 = np.concatenate([np.asarray(rr["lnT"], np.float32) for rr in r],
                         axis=1)

    tmk = _col(_np(inp["L1_attn_tm_k"]).reshape(D))
    tmv = _col(_np(inp["L1_attn_tm_v"]).reshape(D))
    tmr = _col(_np(inp["L1_attn_tm_r"]).reshape(D))
    wdec = _col(_np(inp["L1_attn_time_decay"]).reshape(D))
    wu = _col(_np(inp["L1_attn_time_first"]).reshape(D))
    lng = _col(_np(inp["L1_attn_ln_g"]).reshape(D))
    lnb = _col(_np(inp["L1_attn_ln_b"]).reshape(D))
    wkT = np.ascontiguousarray(_np(inp["L1_attn_Wk"]).T)
    wvT = np.ascontiguousarray(_np(inp["L1_attn_Wv"]).T)
    wrT = np.ascontiguousarray(_np(inp["L1_attn_Wr"]).T)
    woT = np.ascontiguousarray(_np(inp["L1_attn_Wo"]).T)

    ims = []
    for c in range(NC):
        xp = (_col(ln4[:, c * TC - 1]) if c > 0
              else np.zeros((128, NCH), np.float32))
        ims.append(dict(xT=np.ascontiguousarray(ln4[:, c * TC:(c + 1) * TC]),
                        xprev=xp, wkT=wkT, wvT=wvT, wrT=wrT,
                        tmk=tmk, tmv=tmv, tmr=tmr, wdec=wdec))
    ra = _run(prog_wkva(), ims)

    Atot = [np.asarray(rr["oA"], np.float32)[:, TC - 1] for rr in ra]
    Btot = [np.asarray(rr["oB"], np.float32)[:, TC - 1] for rr in ra]
    At = np.zeros((128, NCH * NC), np.float32)
    Bt = np.zeros((128, NCH * NC), np.float32)
    for cc in range(NCH):
        for ii in range(NC):
            At[:, cc * NC + ii] = Atot[ii][cc * 128:(cc + 1) * 128]
            Bt[:, cc * NC + ii] = Btot[ii][cc * 128:(cc + 1) * 128]
    rampb = np.tile(np.arange(TC, dtype=np.float32)[None, :], (128, 1))

    ims = []
    for c in range(NC):
        cim = np.zeros((128, NC), np.float32)
        for ii in range(NC):
            cim[:, ii] = 0.0 if ii < c else -1e30
        ims.append(dict(
            xT=np.ascontiguousarray(x4T[:, c * TC:(c + 1) * TC]),
            iA=np.asarray(ra[c]["oA"]), iB=np.asarray(ra[c]["oB"]),
            ikap=np.asarray(ra[c]["okap"]), iv=np.asarray(ra[c]["ov"]),
            ir=np.asarray(ra[c]["orr"]),
            wdec=wdec, wu=wu, lng=lng, lnb=lnb, woT=woT,
            iAt=At, iBt=Bt, cim=cim, rampb=rampb))
    rb = _run(prog_wkvb(), ims)
    x5T = np.concatenate([np.asarray(rr["yT"], np.float32) for rr in rb],
                         axis=1)

    x6T = run_moe(x5T)
    return np.ascontiguousarray(x6T.T).reshape(1, T, D)



NE = 8
NF = 1024 // 128
FGS = [(0, 2), (2, 4), (4, 6), (6, 8)]


@cached("moe")
def prog_moe():
    nc = bass.Bass("TRN2", target_bir_lowering=False, debug=False)
    xT = nc.dram_tensor("xT", [D, TC], F32, kind="ExternalInput").ap()
    wgT = nc.dram_tensor("wgT", [D, NE], F32, kind="ExternalInput").ap()
    w1T = nc.dram_tensor("w1T", [NE, D, 1024], F32, kind="ExternalInput").ap()
    w3T = nc.dram_tensor("w3T", [NE, D, 1024], F32, kind="ExternalInput").ap()
    w2T = nc.dram_tensor("w2T", [NE, 1024, D], F32, kind="ExternalInput").ap()
    bias = nc.dram_tensor("bias", [1, NE], F32, kind="ExternalInput").ap()
    sel8 = nc.dram_tensor("sel8", [NE, NE * 128], BF16, kind="ExternalInput").ap()
    ident = nc.dram_tensor("ident", [128, 128], BF16, kind="ExternalInput").ap()
    yT = nc.dram_tensor("yT", [D, TC], F32, kind="ExternalOutput").ap()
    TH = 512
    NTT = TH // 128

    with ExitStack() as es:
        e = es.enter_context
        p = Prog(nc, es)
        s_x = [e(nc.sbuf_tensor(f"s_x{b}", [128, TH], F32)) for b in range(2)]
        s_xln = e(nc.sbuf_tensor("s_xln", [128, NCH * TH], BF16))
        s_g = e(nc.sbuf_tensor("s_g", [128, NE * NF * TH], BF16))
        s_sq = e(nc.sbuf_tensor("s_sq", [128, TH], F32))
        s_xb = e(nc.sbuf_tensor("s_xb", [128, TH], BF16))
        s_sqb = e(nc.sbuf_tensor("s_sqb", [128, TH], BF16))
        s_mb = e(nc.sbuf_tensor("s_mb", [128, TH], F32))
        s_rb = e(nc.sbuf_tensor("s_rb", [128, TH], F32))
        s_onesM = e(nc.sbuf_tensor("s_onesM", [128, 128], BF16))
        s_ones1b = e(nc.sbuf_tensor("s_ones1b", [1, 128], BF16))
        s_brow = e(nc.sbuf_tensor("s_brow", [1, NE], F32))
        s_bhi = e(nc.sbuf_tensor("s_bhi", [1, NE], BF16))
        s_blo = e(nc.sbuf_tensor("s_blo", [1, NE], BF16))
        s_blo32 = e(nc.sbuf_tensor("s_blo32", [1, NE], F32))
        s_wgb = e(nc.sbuf_tensor("s_wgb", [128, NCH * NE], BF16))
        s_sel = e(nc.sbuf_tensor("s_sel", [NE, NE * 128], BF16))
        s_biasb = e(nc.sbuf_tensor("s_biasb", [128, 4 * NE], F32))
        s_rw = e(nc.sbuf_tensor("s_rw", [128, 4 * NE], F32))
        s_sc = e(nc.sbuf_tensor("s_sc", [128, 4 * NE], F32))
        s_t1 = e(nc.sbuf_tensor("s_t1", [128, 4 * NE], F32))
        s_m12 = e(nc.sbuf_tensor("s_m12", [128, 8], F32))
        s_we = e(nc.sbuf_tensor("s_we", [128, 4 * NE], F32))
        s_web16 = e(nc.sbuf_tensor("s_web16", [128, 4 * NE], BF16))
        s_weT = e(nc.sbuf_tensor("s_weT", [NE, NTT * 128], BF16))
        s_webc = e(nc.sbuf_tensor("s_webc", [128, TH], F32))
        s_sil = e(nc.sbuf_tensor("s_sil", [128, 3 * TH], BF16))
        s_h3w = e(nc.sbuf_tensor("s_h3w", [128, TH], BF16))
        s_w1 = [e(nc.sbuf_tensor(f"s_w1{b}", [128, 256], BF16)) for b in range(2)]
        s_w3 = [e(nc.sbuf_tensor(f"s_w3{b}", [128, 256], BF16)) for b in range(2)]
        s_w2 = [e(nc.sbuf_tensor(f"s_w2{b}", [128, 256], BF16)) for b in range(2)]
        ps = [e(nc.psum_tensor(f"ps{b}", [128, 512], F32)) for b in range(6)]
        ps_tr = [e(nc.psum_tensor(f"ps_tr{b}", [16, 1024], BF16)) for b in range(2)]
        blk = e(nc.Block())

        @blk.vector
        def _(v):
            v.memset(s_onesM[:], 1.0)
            p.inc(v.memset(s_ones1b[:], 1.0), "consts", 1)

        s_id = e(nc.sbuf_tensor("s_id", [128, 128], BF16))

        @blk.sync
        def _(sync):
            p.inc(sync.dma_start(out=s_sel[:], in_=sel8), "ldsel", 16)
            p.inc(sync.dma_start(out=s_id[:], in_=ident), "ldid", 16)
            p.inc(sync.dma_start(out=s_brow[:], in_=bias), "ldbias", 16)

        @blk.gpsimd
        def _(g):
            for c in range(NCH):
                p.inc(g.dma_start(out=s_wgb[:, c * NE:(c + 1) * NE],
                                  in_=wgT[c * 128:(c + 1) * 128, :]),
                      f"ldwg{c % 2}", 16)

        @blk.vector
        def _(v):
            v.wait_ge(p.sem("ldbias"), 16)
            p.chain(v, v.tensor_copy(s_bhi[:], s_brow[:]))
            p.chain(v, v.tensor_tensor(s_blo32[:], s_brow[:], s_bhi[:],
                                       ALU.subtract))
            p.inc(v.tensor_copy(s_blo[:], s_blo32[:]), "bias2", 1)

        @blk.tensor
        def _(t):
            p.wait(t, "consts")
            p.wait(t, "bias2")
            p.inc(t.matmul(ps[5][:, 0:NE], s_ones1b[:, 0:128],
                           s_bhi[0:1, 0:NE], start=True, stop=False), "biasmm", 1)
            p.wait(t, "biasmm")
            p.inc(t.matmul(ps[5][:, 0:NE], s_ones1b[:, 0:128],
                           s_blo[0:1, 0:NE], start=False, stop=True), "biasmm", 1)

        @blk.vector
        def _(v):
            v.wait_ge(p.sem("biasmm"), 2)
            for tt in range(4):
                p.inc(v.tensor_copy(s_biasb[:, tt * NE:(tt + 1) * NE],
                                    ps[5][:, 0:NE]), "biasb", 1)
                p.wait(v, "biasb")

        for H in range(2):
            tg = f"moe{H}"
            hofs = H * TH

            def loads_x(pid, H=H, tg=tg, hofs=hofs):
                @blk.sync
                def _(sync):
                    for c in range(NCH):
                        if c >= 2:
                            sync.wait_ge(p.sem(f"xu{tg}{pid}"), c - 1)
                            if pid == 0:
                                sync.wait_ge(p.sem(f"sq{tg}"), c - 1)
                        elif pid == 1:
                            sync.wait_ge(p.sem(f"sq{tg}"), NCH)
                            sync.wait_ge(p.sem(f"xu{tg}0"), NCH)
                        elif H == 1 and pid == 0:
                            sync.wait_ge(p.sem("moe0done"), 1)
                        p.inc(sync.dma_start(
                            out=s_x[c % 2][:],
                            in_=xT[c * 128:(c + 1) * 128, hofs:hofs + TH]),
                            f"x{tg}{pid}{c % 2}", 16)

            if H == 1:
                @blk.tensor
                def _(t):
                    t.wait_ge(p.sem("yrmoe0"), NCH)
                    t.wait_ge(p.sem("webcmoe0"), NE)
                    t.wait_ge(p.sem("gdonemoe0"), NE * NF)
                    t.wait_ge(p.sem("silmoe0"), NE * NF)
                    t.wait_ge(p.sem("rsigmoe0"), 4)
                    t.wait_ge(p.sem("biasb"), 4)

            emit_ln(p, blk, tg, NCH, TH, loads_x, s_x,
                    s_sq, s_xb, s_sqb, s_mb, s_rb, s_onesM, ps[0], ps[1])

            loads_x(1)

            @blk.vector
            def _(v, tg=tg):
                v.wait_ge(p.sem(f"rstd{tg}"), 1)
                for c in range(NCH):
                    v.wait_ge(p.sem(f"x{tg}1{c % 2}"), 16 * (c // 2 + 1))
                    p.chain(v, v.tensor_tensor(s_sq[:], s_x[c % 2][:], s_mb[:],
                                               ALU.subtract))
                    ins = v.tensor_tensor(s_xln[:, c * TH:(c + 1) * TH],
                                          s_sq[:], s_rb[:], ALU.mult)
                    p.inc(ins, f"xu{tg}1", 1)
                    p.wait(v, f"xu{tg}1")

            @blk.tensor
            def _(t, tg=tg):
                t.wait_ge(p.sem(f"xu{tg}1"), NCH)
                t.wait_ge(p.sem("ldwg0"), 16 * (NCH // 2))
                t.wait_ge(p.sem("ldwg1"), 16 * (NCH // 2))
                for tt in range(4):
                    if tt > 0:
                        t.wait_ge(p.sem(f"rsig{tg}"), tt)
                    for c in range(NCH):
                        ins = t.matmul(
                            ps[4][:, 0:NE],
                            s_xln[:, c * TH + tt * 128:c * TH + (tt + 1) * 128],
                            s_wgb[:, c * NE:(c + 1) * NE],
                            start=(c == 0), stop=(c == NCH - 1))
                    p.inc(ins, f"rmm{tg}", 1)

            @blk.scalar
            def _(a, tg=tg):
                for tt in range(4):
                    a.wait_ge(p.sem(f"rmm{tg}"), tt + 1)
                    p.inc(a.activation(s_rw[:, tt * NE:(tt + 1) * NE],
                                       ps[4][:, 0:NE], AF.Sigmoid),
                          f"rsig{tg}", 1)

            @blk.vector
            def _(v, tg=tg):
                v.wait_ge(p.sem(f"rsig{tg}"), 4)
                p.wait(v, "biasb")
                p.chain(v, v.tensor_tensor(s_sc[:, 0:4 * NE], s_rw[:, 0:4 * NE],
                                           s_biasb[:, 0:4 * NE], ALU.add))
                for tt in range(4):
                    blkc = slice(tt * NE, (tt + 1) * NE)
                    p.chain(v, v.tensor_reduce(s_m12[:, 0:1], s_sc[:, blkc],
                                               mybir.AxisListType.X, ALU.max))
                    p.chain(v, v.tensor_scalar(s_t1[:, blkc], s_sc[:, blkc],
                                               s_m12[:, 0:1], None, ALU.is_equal))
                    p.chain(v, v.scalar_tensor_tensor(s_sc[:, blkc], s_t1[:, blkc],
                                                      -1.0e30, s_sc[:, blkc],
                                                      ALU.mult, ALU.add))
                    p.chain(v, v.tensor_reduce(s_m12[:, 1:2], s_sc[:, blkc],
                                               mybir.AxisListType.X, ALU.max))
                    p.chain(v, v.tensor_scalar(s_sc[:, blkc], s_sc[:, blkc],
                                               s_m12[:, 1:2], None, ALU.is_equal))
                    p.chain(v, v.tensor_tensor(s_t1[:, blkc], s_t1[:, blkc],
                                               s_sc[:, blkc], ALU.add))
                    p.chain(v, v.tensor_tensor(s_we[:, blkc], s_rw[:, blkc],
                                               s_t1[:, blkc], ALU.mult))
                    p.chain(v, v.tensor_reduce(s_m12[:, 2:3], s_we[:, blkc],
                                               mybir.AxisListType.X, ALU.add))
                    p.chain(v, v.reciprocal(s_m12[:, 3:4], s_m12[:, 2:3]))
                    p.chain(v, v.tensor_scalar(s_we[:, blkc], s_we[:, blkc],
                                               s_m12[:, 3:4], 0.5,
                                               ALU.mult, ALU.mult))
                p.inc(v.tensor_copy(s_web16[:, 0:4 * NE], s_we[:, 0:4 * NE]),
                      f"web{tg}", 1)

            @blk.tensor
            def _(t, tg=tg):
                p.wait(t, f"web{tg}")
                t.wait_ge(p.sem(f"rsig{tg}"), 4)
                for tt in range(4):
                    if tt >= 2:
                        t.wait_ge(p.sem(f"wetv{tg}"), tt - 1)
                    if tt == 0:
                        t.wait_ge(p.sem("ldid"), 16)
                    t.transpose(ps_tr[tt % 2][0:NE, 0:128],
                                s_web16[:, tt * NE:(tt + 1) * NE], s_id[:])
                    p.inc(t.matmul(ps[4][:, 0:1], s_onesM[:, 0:128],
                                   s_onesM[:, 0:1], start=True, stop=True),
                          f"wet{tg}", 1)

            @blk.vector
            def _(v, tg=tg):
                for tt in range(4):
                    v.wait_ge(p.sem(f"wet{tg}"), tt + 1)
                    p.inc(v.tensor_copy(s_weT[0:NE, tt * 128:(tt + 1) * 128],
                                        ps_tr[tt % 2][0:NE, 0:128]),
                          f"wetv{tg}", 1)

            @blk.gpsimd
            def _(g, tg=tg):
                j = 0
                for ex in range(NE):
                    for gi, (f0, f1) in enumerate(FGS):
                        w = (f1 - f0) * 128
                        for c in range(NCH):
                            if j >= 2:
                                g.wait_ge(p.sem(f"w13c{tg}"), j - 1)
                            p.inc(g.dma_start(
                                out=s_w1[j % 2][:, 0:w],
                                in_=w1T[ex, c * 128:(c + 1) * 128,
                                        f0 * 128:f1 * 128]),
                                f"ldw1{tg}{j % 2}", 16)
                            p.inc(g.dma_start(
                                out=s_w3[j % 2][:, 0:w],
                                in_=w3T[ex, c * 128:(c + 1) * 128,
                                        f0 * 128:f1 * 128]),
                                f"ldw1{tg}{j % 2}", 16)
                            j += 1

            @blk.tensor
            def _(t, tg=tg):
                t.wait_ge(p.sem("ldsel"), 16)
                t.wait_ge(p.sem(f"wetv{tg}"), 4)
                j = 0
                for ex in range(NE):
                    if ex > 0:
                        t.wait_ge(p.sem(f"webc{tg}"), ex)
                    for tt in range(4):
                        ins = t.matmul(ps[4][:, tt * 128:(tt + 1) * 128],
                                       s_sel[:, ex * 128:(ex + 1) * 128],
                                       s_weT[0:NE, tt * 128:(tt + 1) * 128],
                                       start=True, stop=True)
                        p.inc(ins, f"webmm{tg}", 1)
                        p.wait(t, f"webmm{tg}")
                    for gi, (f0, f1) in enumerate(FGS):
                        nf = f1 - f0
                        prior = 8 * ex + f0
                        if prior > 0:
                            t.wait_ge(p.sem(f"gdone{tg}"), prior)
                        for c in range(NCH):
                            t.wait_ge(p.sem(f"ldw1{tg}{j % 2}"), 32 * (j // 2 + 1))
                            for f in range(nf):
                                t.matmul(ps[f][:, 0:512],
                                         s_w1[j % 2][:, f * 128:(f + 1) * 128],
                                         s_xln[:, c * TH:(c + 1) * TH],
                                         start=(c == 0), stop=(c == NCH - 1))
                                ins = t.matmul(ps[2 + f][:, 0:512],
                                               s_w3[j % 2][:, f * 128:(f + 1) * 128],
                                               s_xln[:, c * TH:(c + 1) * TH],
                                               start=(c == 0), stop=(c == NCH - 1))
                            p.inc(ins, f"w13c{tg}", 1)
                            j += 1

            @blk.scalar
            def _(a, tg=tg):
                k = 0
                for ex in range(NE):
                    for gi, (f0, f1) in enumerate(FGS):
                        nf = f1 - f0
                        k += NCH
                        a.wait_ge(p.sem(f"w13c{tg}"), k)
                        for f in range(nf):
                            if 8 * ex + f0 + f > 0:
                                a.wait_ge(p.sem(f"gdone{tg}"), 8 * ex + f0 + f)
                            p.inc(a.activation(s_sil[:, f * TH:(f + 1) * TH],
                                               ps[f][:, 0:512], AF.Silu),
                                  f"sil{tg}", 1)

            @blk.vector
            def _(v, tg=tg):
                nsil = 0
                for ex in range(NE):
                    v.wait_ge(p.sem(f"webmm{tg}"), 4 * (ex + 1))
                    p.inc(v.tensor_copy(s_webc[:, 0:TH], ps[4][:, 0:TH]),
                          f"webc{tg}", 1)
                    p.wait(v, f"webc{tg}")
                    for gi, (f0, f1) in enumerate(FGS):
                        nf = f1 - f0
                        for f in range(nf):
                            nsil += 1
                            v.wait_ge(p.sem(f"sil{tg}"), nsil)
                            p.chain(v, v.tensor_tensor(s_h3w[:, 0:TH],
                                                       ps[2 + f][:, 0:512],
                                                       s_webc[:, 0:TH], ALU.mult))
                            fc = f0 + f
                            ins = v.tensor_tensor(
                                s_g[:, (ex * NF + fc) * TH:(ex * NF + fc + 1) * TH],
                                s_sil[:, f * TH:(f + 1) * TH], s_h3w[:, 0:TH],
                                ALU.mult)
                            p.inc(ins, f"gdone{tg}", 1)
                            p.wait(v, f"gdone{tg}")

            @blk.gpsimd
            def _(g, tg=tg):
                for hp in range(8):
                    for ex in range(NE):
                        for f in range(NF):
                            j = (hp * NE + ex) * NF + f
                            if j >= 2:
                                g.wait_ge(p.sem(f"w2c{tg}"), j - 1)
                            p.inc(g.dma_start(
                                out=s_w2[j % 2][:, 0:256],
                                in_=w2T[ex, f * 128:(f + 1) * 128,
                                        hp * 256:(hp + 1) * 256]),
                                f"ldw2{tg}{j % 2}", 16)

            @blk.tensor
            def _(t, tg=tg):
                t.wait_ge(p.sem(f"gdone{tg}"), NE * NF)
                for hp in range(8):
                    if hp > 0:
                        t.wait_ge(p.sem(f"yr{tg}"), 2 * hp)
                    for ex in range(NE):
                        for f in range(NF):
                            j = (hp * NE + ex) * NF + f
                            t.wait_ge(p.sem(f"ldw2{tg}{j % 2}"), 16 * (j // 2 + 1))
                            first = (ex == 0 and f == 0)
                            last = (ex == NE - 1 and f == NF - 1)
                            gsl = s_g[:, (ex * NF + f) * TH:(ex * NF + f + 1) * TH]
                            t.matmul(ps[0][:, 0:512],
                                     s_w2[j % 2][:, 0:128], gsl,
                                     start=first, stop=last)
                            ins = t.matmul(ps[1][:, 0:512],
                                           s_w2[j % 2][:, 128:256], gsl,
                                           start=first, stop=last)
                            p.inc(ins, f"w2c{tg}", 1)

            @blk.vector
            def _(v, tg=tg):
                for hp in range(8):
                    v.wait_ge(p.sem(f"w2c{tg}"), (hp + 1) * NE * NF)
                    for hc in range(2):
                        hid = hp * 2 + hc
                        v.wait_ge(p.sem(f"ldxr{tg}{hid % 2}"), 16 * (hid // 2 + 1))
                        p.inc(v.tensor_tensor(s_x[hid % 2][:, 0:TH],
                                              ps[hc][:, 0:512],
                                              s_x[hid % 2][:, 0:TH], ALU.add),
                              f"yr{tg}", 1)
                        p.wait(v, f"yr{tg}")

            @blk.sync
            def _(sync, tg=tg, hofs=hofs, H=H):
                for hid in range(NCH):
                    sync.wait_ge(p.sem(f"w2c{tg}"), (hid // 2 + 1) * NE * NF)
                    if hid >= 2:
                        sync.wait_ge(p.sem(f"sty{tg}{hid % 2}"), 16 * (hid // 2))
                    p.inc(sync.dma_start(out=s_x[hid % 2][:],
                                         in_=xT[hid * 128:(hid + 1) * 128,
                                                hofs:hofs + TH]),
                          f"ldxr{tg}{hid % 2}", 16)
                    sync.wait_ge(p.sem(f"yr{tg}"), hid + 1)
                    p.inc(sync.dma_start(out=yT[hid * 128:(hid + 1) * 128,
                                                hofs:hofs + TH],
                                         in_=s_x[hid % 2][:]),
                          f"sty{tg}{hid % 2}", 16)
                sync.wait_ge(p.sem(f"sty{tg}0"), 16 * (NCH // 2))
                sync.wait_ge(p.sem(f"sty{tg}1"), 16 * (NCH // 2))
                if H == 0:
                    p.inc(sync.nop(), "moe0done", 1)

    return nc


SEL8 = np.zeros((NE, NE * 128), dtype=np.float32)
for _e in range(NE):
    SEL8[_e, _e * 128:(_e + 1) * 128] = 1.0
SEL8 = SEL8.astype(BF)



@cached("wkva")
def prog_wkva():
    nc = bass.Bass("TRN2", target_bir_lowering=False, debug=False)
    xT = nc.dram_tensor("xT", [D, TC], F32, kind="ExternalInput").ap()
    xprev = nc.dram_tensor("xprev", [128, NCH], F32, kind="ExternalInput").ap()
    wkT = nc.dram_tensor("wkT", [D, D], F32, kind="ExternalInput").ap()
    wvT = nc.dram_tensor("wvT", [D, D], F32, kind="ExternalInput").ap()
    wrT = nc.dram_tensor("wrT", [D, D], F32, kind="ExternalInput").ap()
    tmk = nc.dram_tensor("tmk", [128, NCH], F32, kind="ExternalInput").ap()
    tmv = nc.dram_tensor("tmv", [128, NCH], F32, kind="ExternalInput").ap()
    tmr = nc.dram_tensor("tmr", [128, NCH], F32, kind="ExternalInput").ap()
    wdec = nc.dram_tensor("wdec", [128, NCH], F32, kind="ExternalInput").ap()
    oA = nc.dram_tensor("oA", [D, TC], F32, kind="ExternalOutput").ap()
    oB = nc.dram_tensor("oB", [D, TC], F32, kind="ExternalOutput").ap()
    okap = nc.dram_tensor("okap", [D, TC], BF16, kind="ExternalOutput").ap()
    ov = nc.dram_tensor("ov", [D, TC], BF16, kind="ExternalOutput").ap()
    orr = nc.dram_tensor("orr", [D, TC], BF16, kind="ExternalOutput").ap()
    TF = TC
    rA = oA.rearrange("(n p) t -> n p t", p=128)
    rB = oB.rearrange("(n p) t -> n p t", p=128)
    rk = okap.rearrange("(n p) t -> n p t", p=128)
    rv = ov.rearrange("(n p) t -> n p t", p=128)
    rr_ = orr.rearrange("(n p) t -> n p t", p=128)

    with ExitStack() as es:
        e = es.enter_context
        p = Prog(nc, es)
        s_x = [e(nc.sbuf_tensor(f"s_x{b}", [128, TF], F32)) for b in range(2)]
        s_xm = {m: e(nc.sbuf_tensor(f"s_x{m}", [128, NCH * TF], BF16))
                for m in ("k", "v", "r")}
        s_tm = {m: e(nc.sbuf_tensor(f"s_tm{m}", [128, NCH], F32))
                for m in ("k", "v", "r")}
        s_om = {m: e(nc.sbuf_tensor(f"s_om{m}", [128, NCH], F32))
                for m in ("k", "v", "r")}
        s_xp = e(nc.sbuf_tensor("s_xp", [128, NCH], F32))
        s_t1 = e(nc.sbuf_tensor("s_t1", [128, TF], F32))
        s_wdec = e(nc.sbuf_tensor("s_wdec", [128, NCH], F32))
        s_lam = e(nc.sbuf_tensor("s_lam", [128, NCH], F32))
        s_lrep = e(nc.sbuf_tensor("s_lrep", [128, TF], F32))
        s_v = e(nc.sbuf_tensor("s_v", [128, NCH * TF], BF16))
        s_kap = [e(nc.sbuf_tensor(f"s_kap{b}", [128, TF], BF16)) for b in range(2)]
        s_r = [e(nc.sbuf_tensor(f"s_r{b}", [128, TF], BF16)) for b in range(2)]
        s_kf = e(nc.sbuf_tensor("s_kf", [128, TF], F32))
        s_kv = e(nc.sbuf_tensor("s_kv", [128, TF], F32))
        s_A = [e(nc.sbuf_tensor(f"s_A{b}", [128, TF], F32)) for b in range(2)]
        s_B = [e(nc.sbuf_tensor(f"s_B{b}", [128, TF], F32)) for b in range(2)]
        s_w = [e(nc.sbuf_tensor(f"s_w{b}", [128, 256], BF16)) for b in range(2)]
        ps2 = [e(nc.psum_tensor(f"ps{b}", [128, TF], F32)) for b in range(2)]
        blk = e(nc.Block())

        @blk.sync
        def _(sync):
            for ap_, dst in ((tmk, s_tm["k"]), (tmv, s_tm["v"]), (tmr, s_tm["r"]),
                             (xprev, s_xp), (wdec, s_wdec)):
                p.inc(sync.dma_start(out=dst[:], in_=ap_), "ldc", 16)
            for c in range(NCH):
                if c >= 2:
                    sync.wait_ge(p.sem("xmdone"), 3 * (c - 1))
                p.inc(sync.dma_start(out=s_x[c % 2][:],
                                     in_=xT[c * 128:(c + 1) * 128, :]),
                      f"ldx{c % 2}", 16)

        @blk.scalar
        def _(a):
            a.wait_ge(p.sem("ldc"), 80)
            p.inc(a.activation(s_lam[:], s_wdec[:], AF.Exp), "lam0", 1)
            a.wait_ge(p.sem("lam0"), 1)
            p.inc(a.activation(s_lam[:], s_lam[:], AF.Exp, scale=0.0), "lamr", 1)

        @blk.vector
        def _(v):
            v.wait_ge(p.sem("ldc"), 80)
            for m in ("k", "v", "r"):
                p.chain(v, v.tensor_scalar(s_om[m][:], s_tm[m][:], -1.0, 1.0,
                                           ALU.mult, ALU.add))
            for c in range(NCH):
                v.wait_ge(p.sem(f"ldx{c % 2}"), 16 * (c // 2 + 1))
                for m in ("k", "v", "r"):
                    p.chain(v, v.tensor_scalar(s_t1[:], s_x[c % 2][:],
                                               s_tm[m][:, c:c + 1], None,
                                               ALU.mult))
                    dst = s_xm[m][:, c * TF:(c + 1) * TF]
                    v.scalar_tensor_tensor(dst[:, 1:TF], s_x[c % 2][:, 0:TF - 1],
                                           s_om[m][:, c:c + 1], s_t1[:, 1:TF],
                                           ALU.mult, ALU.add)
                    ins = v.scalar_tensor_tensor(dst[:, 0:1], s_xp[:, c:c + 1],
                                                 s_om[m][:, c:c + 1],
                                                 s_t1[:, 0:1],
                                                 ALU.mult, ALU.add)
                    p.inc(ins, "xmdone", 1)
                    p.wait(v, "xmdone")

        projs = [("v", wvT), ("k", wkT), ("r", wrT)]

        @blk.gpsimd
        def _(g):
            j = 0
            for m, wap in projs:
                for pg in range(8):
                    for c in range(NCH):
                        if j >= 2:
                            g.wait_ge(p.sem("wcons"), j - 1)
                        p.inc(g.dma_start(out=s_w[j % 2][:, 0:256],
                                          in_=wap[c * 128:(c + 1) * 128,
                                                  pg * 256:(pg + 1) * 256]),
                              f"ldw{j % 2}", 16)
                        j += 1

        @blk.tensor
        def _(t):
            t.wait_ge(p.sem("xmdone"), 3 * NCH)
            j = 0
            for mi, (m, wap) in enumerate(projs):
                for pg in range(8):
                    pi = mi * 8 + pg
                    if pi > 0:
                        t.wait_ge(p.sem("pact"), 2 * pi)
                    for c in range(NCH):
                        t.wait_ge(p.sem(f"ldw{j % 2}"), 16 * (j // 2 + 1))
                        for half in range(2):
                            for f0 in (0, 512):
                                ins = t.matmul(
                                    ps2[half][:, f0:f0 + 512],
                                    s_w[j % 2][:, half * 128:(half + 1) * 128],
                                    s_xm[m][:, c * TF + f0:c * TF + f0 + 512],
                                    start=(c == 0), stop=(c == NCH - 1))
                        p.inc(ins, "wcons", 1)
                        j += 1

        @blk.scalar
        def _(a):
            for mi, (m, wap) in enumerate(projs):
                for pg in range(8):
                    pi = mi * 8 + pg
                    a.wait_ge(p.sem("wcons"), (pi + 1) * NCH)
                    for half in range(2):
                        if m in ("v", "k"):
                            if pi > 0:
                                a.wait_ge(p.sem("pev"), 2 * pi + half - 1)
                            if m == "k" and pg > 0:
                                a.wait_ge(p.sem(f"kapst{half}"), 16 * pg)
                            ins = a.activation(s_kap[half][:, 0:TF],
                                               ps2[half][:, 0:TF],
                                               AF.Copy if m == "v" else AF.Exp)
                        else:
                            if pg > 0:
                                a.wait_ge(p.sem(f"rst{half}"), 16 * pg)
                            ins = a.activation(s_r[half][:, 0:TF],
                                               ps2[half][:, 0:TF], AF.Sigmoid)
                        p.inc(ins, "pact", 1)

        @blk.vector
        def _(v):
            npa = 0
            for mi, (m, wap) in enumerate(projs):
                for pg in range(8):
                    for half in range(2):
                        dc = pg * 2 + half
                        npa += 1
                        v.wait_ge(p.sem("pact"), npa)
                        if m == "v":
                            ins = v.tensor_copy(s_v[:, dc * TF:(dc + 1) * TF],
                                                s_kap[half][:, 0:TF])
                            p.inc(ins, "pev", 1)
                            p.wait(v, "pev")
                        elif m == "k":
                            p.chain(v, v.tensor_copy(s_kf[:],
                                                     s_kap[half][:, 0:TF]))
                            v.wait_ge(p.sem("lamr"), 1)
                            p.chain(v, v.tensor_copy(
                                s_lrep[:],
                                s_lam[:, dc:dc + 1].broadcast_to((128, TF))))
                            p.chain(v, v.tensor_tensor(
                                s_kv[:], s_kf[:], s_v[:, dc * TF:(dc + 1) * TF],
                                ALU.mult))
                            if dc >= 2:
                                v.wait_ge(p.sem(f"abst{dc % 2}"), 32 * (dc // 2))
                            p.chain(v, v.tensor_tensor_scan(
                                s_A[dc % 2][:, 0:TF], s_lrep[:], s_kv[:], 0.0,
                                ALU.mult, ALU.add))
                            p.inc(v.tensor_tensor_scan(
                                s_B[dc % 2][:, 0:TF], s_lrep[:], s_kf[:], 0.0,
                                ALU.mult, ALU.add), "pev", 1)
                            p.wait(v, "pev")

        @blk.sync
        def _(sync):
            for dc in range(NCH):
                sync.wait_ge(p.sem("pev"), dc + 1)
                p.inc(sync.dma_start(out=rv[dc, :, :],
                                     in_=s_v[:, dc * TF:(dc + 1) * TF]),
                      f"vst{dc % 2}", 16)
            for dc in range(NCH):
                half = dc % 2
                sync.wait_ge(p.sem("pev"), NCH + dc + 1)
                p.inc(sync.dma_start(out=rk[dc, :, :],
                                     in_=s_kap[half][:, 0:TF]),
                      f"kapst{half}", 16)
                p.inc(sync.dma_start(out=rA[dc, :, :], in_=s_A[half][:, 0:TF]),
                      f"abst{half}", 16)
                p.inc(sync.dma_start(out=rB[dc, :, :], in_=s_B[half][:, 0:TF]),
                      f"abst{half}", 16)
            for dc in range(NCH):
                half = dc % 2
                sync.wait_ge(p.sem("pact"), 2 * NCH + dc + 1)
                p.inc(sync.dma_start(out=rr_[dc, :, :], in_=s_r[half][:, 0:TF]),
                      f"rst{half}", 16)
            sync.wait_ge(p.sem("vst0"), 16 * 8)
            sync.wait_ge(p.sem("vst1"), 16 * 8)
            sync.wait_ge(p.sem("kapst0"), 16 * 8)
            sync.wait_ge(p.sem("kapst1"), 16 * 8)
            sync.wait_ge(p.sem("abst0"), 32 * 8)
            sync.wait_ge(p.sem("abst1"), 32 * 8)
            sync.wait_ge(p.sem("rst0"), 16 * 8)
            sync.wait_ge(p.sem("rst1"), 16 * 8)

    return nc



@cached("wkvb")
def prog_wkvb():
    nc = bass.Bass("TRN2", target_bir_lowering=False, debug=False)
    xT = nc.dram_tensor("xT", [D, TC], F32, kind="ExternalInput").ap()
    iA = nc.dram_tensor("iA", [D, TC], F32, kind="ExternalInput").ap()
    iB = nc.dram_tensor("iB", [D, TC], F32, kind="ExternalInput").ap()
    ikap = nc.dram_tensor("ikap", [D, TC], BF16, kind="ExternalInput").ap()
    iv = nc.dram_tensor("iv", [D, TC], BF16, kind="ExternalInput").ap()
    ir = nc.dram_tensor("ir", [D, TC], BF16, kind="ExternalInput").ap()
    wdec = nc.dram_tensor("wdec", [128, NCH], F32, kind="ExternalInput").ap()
    wu = nc.dram_tensor("wu", [128, NCH], F32, kind="ExternalInput").ap()
    lng = nc.dram_tensor("lng", [128, NCH], F32, kind="ExternalInput").ap()
    lnb = nc.dram_tensor("lnb", [128, NCH], F32, kind="ExternalInput").ap()
    woT = nc.dram_tensor("woT", [D, D], F32, kind="ExternalInput").ap()
    iAt = nc.dram_tensor("iAt", [128, NCH * NC], F32, kind="ExternalInput").ap()
    iBt = nc.dram_tensor("iBt", [128, NCH * NC], F32, kind="ExternalInput").ap()
    cim = nc.dram_tensor("cim", [128, NC], F32, kind="ExternalInput").ap()
    rampb = nc.dram_tensor("rampb", [128, TC], F32, kind="ExternalInput").ap()
    yT = nc.dram_tensor("yT", [D, TC], F32, kind="ExternalOutput").ap()
    TF = TC
    rAi = iA.rearrange("(n p) t -> n p t", p=128)
    rBi = iB.rearrange("(n p) t -> n p t", p=128)
    rki = ikap.rearrange("(n p) t -> n p t", p=128)
    rvi = iv.rearrange("(n p) t -> n p t", p=128)
    rri = ir.rearrange("(n p) t -> n p t", p=128)

    with ExitStack() as es:
        e = es.enter_context
        p = Prog(nc, es)
        s_c = {n: e(nc.sbuf_tensor(f"s_{n}", [128, NCH], F32))
               for n in ("wdec", "wu", "lng", "lnb", "d", "eu", "Ap", "Bp", "f")}
        s_At = e(nc.sbuf_tensor("s_At", [128, NCH * NC], F32))
        s_Bt = e(nc.sbuf_tensor("s_Bt", [128, NCH * NC], F32))
        s_cim = e(nc.sbuf_tensor("s_cim", [128, NC], F32))
        s_ramp = e(nc.sbuf_tensor("s_ramp", [128, TF], F32))
        s_psi = e(nc.sbuf_tensor("s_psi", [128, TF], F32))
        s_sd = e(nc.sbuf_tensor("s_sd", [128, TF], F32))
        s_A = [e(nc.sbuf_tensor(f"s_A{b}", [128, TF], F32)) for b in range(2)]
        s_B = [e(nc.sbuf_tensor(f"s_B{b}", [128, TF], F32)) for b in range(2)]
        s_kap = [e(nc.sbuf_tensor(f"s_kap{b}", [128, TF], BF16)) for b in range(2)]
        s_v = [e(nc.sbuf_tensor(f"s_v{b}", [128, TF], BF16)) for b in range(2)]
        s_r = [e(nc.sbuf_tensor(f"s_r{b}", [128, TF], BF16)) for b in range(2)]
        s_ku = e(nc.sbuf_tensor("s_ku", [128, TF], F32))
        s_kuv = e(nc.sbuf_tensor("s_kuv", [128, TF], F32))
        s_n = e(nc.sbuf_tensor("s_n", [128, TF], F32))
        s_den = e(nc.sbuf_tensor("s_den", [128, TF], F32))
        s_wkv = e(nc.sbuf_tensor("s_wkv", [128, NCH * TF], F32))
        s_rw = e(nc.sbuf_tensor("s_rw", [128, NCH * TF], BF16))
        s_sq = e(nc.sbuf_tensor("s_sq", [128, TF], F32))
        s_xb = e(nc.sbuf_tensor("s_xb", [128, TF], BF16))
        s_sqb = e(nc.sbuf_tensor("s_sqb", [128, TF], BF16))
        s_mb = e(nc.sbuf_tensor("s_mb", [128, TF], F32))
        s_rb = e(nc.sbuf_tensor("s_rb", [128, TF], F32))
        s_onesM = e(nc.sbuf_tensor("s_onesM", [128, 128], BF16))
        s_x = [e(nc.sbuf_tensor(f"s_x{b}", [128, TF], F32)) for b in range(2)]
        s_w = [e(nc.sbuf_tensor(f"s_w{b}", [128, 256], BF16)) for b in range(2)]
        ps_sx = e(nc.psum_tensor("ps_sx", [128, TF], F32))
        ps_sq = e(nc.psum_tensor("ps_sq", [128, TF], F32))
        ps_o = [e(nc.psum_tensor(f"ps_o{b}", [128, TF], F32)) for b in range(2)]
        blk = e(nc.Block())

        @blk.vector
        def _(v):
            p.inc(v.memset(s_onesM[:], 1.0), "consts", 1)

        @blk.sync
        def _(sync):
            for ap_, dst in ((wdec, s_c["wdec"]), (wu, s_c["wu"]),
                             (lng, s_c["lng"]), (lnb, s_c["lnb"]),
                             (cim, s_cim), (rampb, s_ramp)):
                p.inc(sync.dma_start(out=dst[:], in_=ap_), "ldc", 16)
            p.inc(sync.dma_start(out=s_At[:], in_=iAt), "ldc", 16)
            p.inc(sync.dma_start(out=s_Bt[:], in_=iBt), "ldc", 16)
            for c in range(NCH):
                if c >= 2:
                    sync.wait_ge(p.sem("wkvc"), c - 1)
                b = c % 2
                p.inc(sync.dma_start(out=s_A[b][:], in_=rAi[c, :, :]), f"ldA{b}", 16)
                p.inc(sync.dma_start(out=s_B[b][:], in_=rBi[c, :, :]), f"ldB{b}", 16)
                p.inc(sync.dma_start(out=s_kap[b][:], in_=rki[c, :, :]), f"ldk{b}", 16)
                p.inc(sync.dma_start(out=s_v[b][:], in_=rvi[c, :, :]), f"ldv{b}", 16)

        @blk.scalar
        def _(a):
            a.wait_ge(p.sem("ldc"), 16 * 8)
            p.inc(a.activation(s_c["d"][:], s_c["wdec"][:], AF.Exp), "dexp", 1)
            a.wait_ge(p.sem("dexp"), 1)
            p.inc(a.activation(s_c["eu"][:], s_c["wu"][:], AF.Exp), "euexp", 1)
            for i in range(NC):
                a.wait_ge(p.sem("pfx"), 2 * i)
                p.inc(a.activation(s_c["f"][:], s_c["d"][:], AF.Exp,
                                   scale=s_cim[:, i:i + 1]), "fexp", 1)

        @blk.vector
        def _(v):
            for i in range(NC):
                v.wait_ge(p.sem("fexp"), i + 1)
                if i == 0:
                    p.chain(v, v.tensor_tensor(s_c["Ap"][:], s_At[:, i::NC],
                                               s_c["f"][:], ALU.mult))
                    p.inc(v.tensor_tensor(s_c["Bp"][:], s_Bt[:, i::NC],
                                          s_c["f"][:], ALU.mult), "pfx", 1)
                    p.wait(v, "pfx")
                    p.inc(v.engine_nop(), "pfx", 1)
                else:
                    p.chain(v, v.tensor_tensor(s_sq[:, 0:NCH], s_At[:, i::NC],
                                               s_c["f"][:], ALU.mult))
                    p.chain(v, v.tensor_tensor(s_c["Ap"][:], s_c["Ap"][:],
                                               s_sq[:, 0:NCH], ALU.add))
                    p.chain(v, v.tensor_tensor(s_sq[:, 0:NCH], s_Bt[:, i::NC],
                                               s_c["f"][:], ALU.mult))
                    p.inc(v.tensor_tensor(s_c["Bp"][:], s_c["Bp"][:],
                                          s_sq[:, 0:NCH], ALU.add), "pfx", 1)
                    p.wait(v, "pfx")
                    p.inc(v.engine_nop(), "pfx", 1)
            p.inc(v.engine_nop(), "pfxdone", 1)

        @blk.vector
        def _(v):
            p.wait(v, "pfxdone")
            for c in range(NCH):
                b = c % 2
                rd = 16 * (c // 2 + 1)
                v.wait_ge(p.sem("dexp"), 1)
                v.wait_ge(p.sem("psie"), c)
                p.inc(v.tensor_scalar(s_sd[:], s_ramp[:],
                                      s_c["d"][:, c:c + 1], None, ALU.mult),
                      "sdrdy", 1)
                p.wait(v, "sdrdy")
                v.wait_ge(p.sem("psie"), c + 1)
                v.wait_ge(p.sem(f"ldk{b}"), rd)
                v.wait_ge(p.sem("euexp"), 1)
                p.chain(v, v.tensor_scalar(s_ku[:], s_kap[b][:],
                                           s_c["eu"][:, c:c + 1], None, ALU.mult))
                v.wait_ge(p.sem(f"ldv{b}"), rd)
                p.chain(v, v.tensor_tensor(s_kuv[:], s_ku[:], s_v[b][:], ALU.mult))
                p.chain(v, v.scalar_tensor_tensor(s_n[:], s_psi[:],
                                                  s_c["Ap"][:, c:c + 1], s_kuv[:],
                                                  ALU.mult, ALU.add))
                v.wait_ge(p.sem(f"ldA{b}"), rd)
                p.chain(v, v.tensor_tensor(s_n[:, 1:TF], s_n[:, 1:TF],
                                           s_A[b][:, 0:TF - 1], ALU.add))
                p.chain(v, v.scalar_tensor_tensor(s_den[:], s_psi[:],
                                                  s_c["Bp"][:, c:c + 1], s_ku[:],
                                                  ALU.mult, ALU.add))
                v.wait_ge(p.sem(f"ldB{b}"), rd)
                ins = v.tensor_tensor(s_den[:, 1:TF], s_den[:, 1:TF],
                                      s_B[b][:, 0:TF - 1], ALU.add)
                p.inc(ins, "denr", 1)
                p.wait(v, "denr")
                v.wait_ge(p.sem("inve"), c + 1)
                ins = v.tensor_tensor(s_wkv[:, c * TF:(c + 1) * TF], s_n[:],
                                      s_den[:], ALU.mult)
                p.inc(ins, "wkvc", 1)
                p.wait(v, "wkvc")

        @blk.scalar
        def _(a):
            for c in range(NCH):
                a.wait_ge(p.sem("sdrdy"), c + 1)
                p.inc(a.activation(s_psi[:], s_sd[:], AF.Exp, scale=0.0),
                      "psie", 1)
                a.wait_ge(p.sem("denr"), c + 1)
                p.inc(a.activation(s_den[:], s_den[:], AF.Ln), "lnd", 1)
                a.wait_ge(p.sem("lnd"), c + 1)
                p.inc(a.activation(s_den[:], s_den[:], AF.Exp, scale=-1.0),
                      "inve", 1)

        @blk.scalar
        def _(a):
            for c in range(NCH):
                a.wait_ge(p.sem("wkvc"), c + 1)
                a.wait_ge(p.sem("sqbw"), c)
                p.inc(a.activation(s_sq[:], s_wkv[:, c * TF:(c + 1) * TF],
                                   AF.Square), "sqw", 1)

        @blk.vector
        def _(v):
            for c in range(NCH):
                v.wait_ge(p.sem("smmw"), c)
                v.wait_ge(p.sem("wkvc"), c + 1)
                p.inc(v.tensor_copy(s_xb[:], s_wkv[:, c * TF:(c + 1) * TF]),
                      "xbw", 1)
                v.wait_ge(p.sem("sqw"), c + 1)
                p.inc(v.tensor_copy(s_sqb[:], s_sq[:]), "sqbw", 1)

        @blk.tensor
        def _(t):
            p.wait(t, "consts")
            for c in range(NCH):
                t.wait_ge(p.sem("sqbw"), c + 1)
                t.wait_ge(p.sem("xbw"), c + 1)
                for f0 in (0, 512):
                    t.matmul(ps_sx[:, f0:f0 + 512], s_onesM[:, 0:128],
                             s_xb[:, f0:f0 + 512],
                             start=(c == 0), stop=(c == NCH - 1))
                    ins = t.matmul(ps_sq[:, f0:f0 + 512], s_onesM[:, 0:128],
                                   s_sqb[:, f0:f0 + 512],
                                   start=(c == 0), stop=(c == NCH - 1))
                p.inc(ins, "smmw", 1)

        @blk.vector
        def _(v):
            v.wait_ge(p.sem("smmw"), NCH)
            p.chain(v, v.tensor_scalar(s_mb[:], ps_sx[:, 0:TF], 1.0 / D, None,
                                       ALU.mult))
            p.chain(v, v.tensor_scalar(s_rb[:], ps_sq[:, 0:TF], 1.0 / D, None,
                                       ALU.mult))
            p.chain(v, v.tensor_tensor(s_sq[:], s_mb[:], s_mb[:], ALU.mult))
            p.chain(v, v.tensor_tensor(s_rb[:], s_rb[:], s_sq[:], ALU.subtract))
            p.inc(v.tensor_scalar(s_rb[:], s_rb[:], EPS, None, ALU.add),
                  "varw", 1)

        @blk.scalar
        def _(a):
            p.wait(a, "varw")
            p.inc(a.activation(s_rb[:], s_rb[:], AF.Ln), "lnw", 1)
            a.wait_ge(p.sem("lnw"), p.cnt["lnw"])
            p.inc(a.activation(s_rb[:], s_rb[:], AF.Exp, scale=-0.5), "rstdw", 1)

        @blk.sync
        def _(sync):
            for c in range(NCH):
                if c >= 2:
                    sync.wait_ge(p.sem("rwc"), c - 1)
                p.inc(sync.dma_start(out=s_r[c % 2][:], in_=rri[c, :, :]),
                      f"ldr{c % 2}", 16)

        @blk.vector
        def _(v):
            v.wait_ge(p.sem("rstdw"), 1)
            for c in range(NCH):
                p.chain(v, v.tensor_tensor(s_sq[:], s_wkv[:, c * TF:(c + 1) * TF],
                                           s_mb[:], ALU.subtract))
                p.chain(v, v.tensor_tensor(s_sq[:], s_sq[:], s_rb[:], ALU.mult))
                p.chain(v, v.tensor_scalar(s_sq[:], s_sq[:],
                                           s_c["lng"][:, c:c + 1],
                                           s_c["lnb"][:, c:c + 1],
                                           ALU.mult, ALU.add))
                v.wait_ge(p.sem(f"ldr{c % 2}"), 16 * (c // 2 + 1))
                ins = v.tensor_tensor(s_rw[:, c * TF:(c + 1) * TF], s_sq[:],
                                      s_r[c % 2][:], ALU.mult)
                p.inc(ins, "rwc", 1)
                p.wait(v, "rwc")

        @blk.gpsimd
        def _(g):
            for hp in range(8):
                for c in range(NCH):
                    j = hp * NCH + c
                    if j >= 2:
                        g.wait_ge(p.sem("wocons"), j - 1)
                    p.inc(g.dma_start(out=s_w[j % 2][:, 0:256],
                                      in_=woT[c * 128:(c + 1) * 128,
                                              hp * 256:(hp + 1) * 256]),
                          f"ldwo{j % 2}", 16)

        @blk.tensor
        def _(t):
            t.wait_ge(p.sem("rwc"), NCH)
            for hp in range(8):
                if hp > 0:
                    t.wait_ge(p.sem("yro"), 2 * hp)
                for c in range(NCH):
                    j = hp * NCH + c
                    t.wait_ge(p.sem(f"ldwo{j % 2}"), 16 * (j // 2 + 1))
                    for half in range(2):
                        for f0 in (0, 512):
                            ins = t.matmul(
                                ps_o[half][:, f0:f0 + 512],
                                s_w[j % 2][:, half * 128:(half + 1) * 128],
                                s_rw[:, c * TF + f0:c * TF + f0 + 512],
                                start=(c == 0), stop=(c == NCH - 1))
                    p.inc(ins, "wocons", 1)

        @blk.sync
        def _(sync):
            for hid in range(NCH):
                sync.wait_ge(p.sem("wocons"), (hid // 2 + 1) * NCH)
                if hid >= 2:
                    sync.wait_ge(p.sem(f"styw{hid % 2}"), 16 * (hid // 2))
                p.inc(sync.dma_start(out=s_x[hid % 2][:],
                                     in_=xT[hid * 128:(hid + 1) * 128, :]),
                      f"ldxw{hid % 2}", 16)
                sync.wait_ge(p.sem("yro"), hid + 1)
                p.inc(sync.dma_start(out=yT[hid * 128:(hid + 1) * 128, :],
                                     in_=s_x[hid % 2][:]), f"styw{hid % 2}", 16)
            sync.wait_ge(p.sem("styw0"), 16 * 8)
            sync.wait_ge(p.sem("styw1"), 16 * 8)

        @blk.vector
        def _(v):
            for hp in range(8):
                v.wait_ge(p.sem("wocons"), (hp + 1) * NCH)
                for half in range(2):
                    hid = hp * 2 + half
                    v.wait_ge(p.sem(f"ldxw{hid % 2}"), 16 * (hid // 2 + 1))
                    ins = v.tensor_tensor(s_x[hid % 2][:, 0:TF],
                                          ps_o[half][:, 0:TF],
                                          s_x[hid % 2][:, 0:TF], ALU.add)
                    p.inc(ins, "yro", 1)
                    p.wait(v, "yro")

    return nc



TE = 1664
NHEAD = 16
NKV = 4
ASCALE = 0.08838834764831845


@cached("attn")
def prog_attn():
    nc = bass.Bass("TRN2", target_bir_lowering=False, debug=False)
    xTe = nc.dram_tensor("xTe", [D, TE], F32, kind="ExternalInput").ap()
    wq = nc.dram_tensor("wq", [D, D], F32, kind="ExternalInput").ap()
    wkv = nc.dram_tensor("wkv", [D, 1024], F32, kind="ExternalInput").ap()
    wo = nc.dram_tensor("wo", [D, D], F32, kind="ExternalInput").ap()
    bqc = nc.dram_tensor("bqc", [128, NCH], F32, kind="ExternalInput").ap()
    bkc = nc.dram_tensor("bkc", [128, 4], F32, kind="ExternalInput").ap()
    bvr = nc.dram_tensor("bvr", [1, 512], BF16, kind="ExternalInput").ap()
    boc = nc.dram_tensor("boc", [128, NCH], F32, kind="ExternalInput").ap()
    cosq = nc.dram_tensor("cosq", [128, TC], F32, kind="ExternalInput").ap()
    sinq = nc.dram_tensor("sinq", [128, TC], F32, kind="ExternalInput").ap()
    cosk = nc.dram_tensor("cosk", [128, TE], F32, kind="ExternalInput").ap()
    sink = nc.dram_tensor("sink", [128, TE], F32, kind="ExternalInput").ap()
    bandm = nc.dram_tensor("bandm", [128, 1408], F32, kind="ExternalInput").ap()
    metam = nc.dram_tensor("metam", [16, 1024], F32, kind="ExternalInput").ap()
    bkill = nc.dram_tensor("bkill", [128, 18], F32, kind="ExternalInput").ap()
    rotm = nc.dram_tensor("rotm", [128, 128], BF16, kind="ExternalInput").ap()
    ones1 = nc.dram_tensor("ones1", [1, 128], BF16, kind="ExternalInput").ap()
    yT = nc.dram_tensor("yT", [D, TC], F32, kind="ExternalOutput").ap()

    with ExitStack() as es:
        e = es.enter_context
        p = Prog(nc, es)
        s_sh = e(nc.sbuf_tensor("s_sh", [128, NCH * TE], BF16))
        s_xln = s_sh[:, 0:NCH * TE]
        s_ao = s_sh[:, 0:NHEAD * 1024]
        s_E = s_sh[:, NHEAD * 1024:NHEAD * 1024 + 9 * 512]
        s_q = e(nc.sbuf_tensor("s_q", [128, NHEAD * 1024], BF16))
        s_k = e(nc.sbuf_tensor("s_k", [128, NKV * TE], BF16))
        s_v = e(nc.sbuf_tensor("s_v", [128, 13 * 512], BF16))
        s_x = [e(nc.sbuf_tensor(f"s_x{b}", [128, TE], F32)) for b in range(2)]
        s_xb = e(nc.sbuf_tensor("s_xb", [128, TE], BF16))
        s_mb = e(nc.sbuf_tensor("s_mb", [128, TE], F32))
        s_rb = e(nc.sbuf_tensor("s_rb", [128, TE], F32))
        s_onesM = e(nc.sbuf_tensor("s_onesM", [128, 128], BF16))
        s_cqb = e(nc.sbuf_tensor("s_cqb", [128, TC], BF16))
        s_sqx = e(nc.sbuf_tensor("s_sqx", [128, TC], BF16))
        s_ckb = e(nc.sbuf_tensor("s_ckb", [128, TE], BF16))
        s_skb = e(nc.sbuf_tensor("s_skb", [128, TE], BF16))
        s_band = e(nc.sbuf_tensor("s_band", [128, 1408], F32))
        s_meta = e(nc.sbuf_tensor("s_meta", [16, 1024], F32))
        s_bkl = e(nc.sbuf_tensor("s_bkl", [128, 18], F32))
        s_rotm = e(nc.sbuf_tensor("s_rotm", [128, 128], BF16))
        s_1r = e(nc.sbuf_tensor("s_1r", [1, 128], BF16))
        s_bq = e(nc.sbuf_tensor("s_bq", [128, NCH], F32))
        s_bkk = e(nc.sbuf_tensor("s_bkk", [128, 4], F32))
        s_bo = e(nc.sbuf_tensor("s_bo", [128, NCH], F32))
        s_bvr = e(nc.sbuf_tensor("s_bvr", [1, 512], BF16))
        s_vbb = e(nc.sbuf_tensor("s_vbb", [128, 512], BF16))
        s_t1 = e(nc.sbuf_tensor("s_t1", [128, TE], F32))
        s_t2 = e(nc.sbuf_tensor("s_t2", [128, TE], F32))
        s_kb = e(nc.sbuf_tensor("s_kb", [128, TE], BF16))
        s_st = e(nc.sbuf_tensor("s_st", [128, 1024], F32))
        s_rec = e(nc.sbuf_tensor("s_rec", [128, 512], F32))
        s_w = [e(nc.sbuf_tensor(f"s_w{b}", [128, 512], BF16)) for b in range(2)]
        ps_a = e(nc.psum_tensor("ps_a", [128, 2048], F32))
        ps_b = e(nc.psum_tensor("ps_b", [128, 2048], F32))
        blk = e(nc.Block())

        Q = [ps_b[:, i * 512:(i + 1) * 512] for i in range(4)]
        CB = [(0, 512), (512, 1024), (1024, 1536), (1536, 1664)]

        @blk.vector
        def _(v):
            p.inc(v.memset(s_onesM[:], 1.0), "consts", 1)

        @blk.sync
        def _(sync):
            for ap_, dst in ((bqc, s_bq), (bkc, s_bkk), (boc, s_bo),
                             (bandm, s_band), (metam, s_meta), (bkill, s_bkl),
                             (rotm, s_rotm), (ones1, s_1r), (bvr, s_bvr)):
                p.inc(sync.dma_start(out=dst[:], in_=ap_), "ldc", 16)

        @blk.gpsimd
        def _(g):
            for ap_, dst in ((cosq, s_cqb), (sinq, s_sqx), (cosk, s_ckb),
                             (sink, s_skb)):
                p.inc(g.dma_start(out=dst[:], in_=ap_), "ldrope", 16)

        def loads_x(pid, gate, gval, consem):
            @blk.sync
            def _(sync):
                for c in range(NCH):
                    if c >= 2:
                        sync.wait_ge(p.sem(consem), c - 1)
                    elif gate is not None:
                        sync.wait_ge(p.sem(gate), gval)
                    p.inc(sync.dma_start(out=s_x[c % 2][:],
                                         in_=xTe[c * 128:(c + 1) * 128, :]),
                          f"xa{pid}{c % 2}", 16)

        loads_x(0, None, 0, "xu0")

        @blk.vector
        def _(v):
            for c in range(NCH):
                v.wait_ge(p.sem("smm0"), c)
                v.wait_ge(p.sem(f"xa0{c % 2}"), 16 * (c // 2 + 1))
                p.inc(v.tensor_copy(s_xb[:], s_x[c % 2][:]), "xu0", 1)

        @blk.tensor
        def _(t):
            p.wait(t, "consts")
            for c in range(NCH):
                t.wait_ge(p.sem("xu0"), c + 1)
                for f0, f1 in CB:
                    ins = t.matmul(ps_a[:, f0:f1], s_onesM[:], s_xb[:, f0:f1],
                                   start=(c == 0), stop=(c == NCH - 1))
                p.inc(ins, "smm0", 1)

        @blk.vector
        def _(v):
            v.wait_ge(p.sem("smm0"), NCH)
            p.inc(v.tensor_scalar(s_mb[:], ps_a[:, 0:TE], 1.0 / D, None,
                                  ALU.mult), "mbr", 1)
            p.wait(v, "mbr")

        loads_x(1, "mbr", 1, "sq1")

        @blk.scalar
        def _(a):
            for c in range(NCH):
                a.wait_ge(p.sem(f"xa1{c % 2}"), 16 * (c // 2 + 1))
                a.wait_ge(p.sem("sqb1"), c)
                p.inc(a.activation(s_t1[:], s_x[c % 2][:], AF.Square), "sq1", 1)

        @blk.vector
        def _(v):
            for c in range(NCH):
                v.wait_ge(p.sem("smm1"), c)
                v.wait_ge(p.sem("sq1"), c + 1)
                p.inc(v.tensor_copy(s_xb[:], s_t1[:]), "sqb1", 1)

        @blk.tensor
        def _(t):
            for c in range(NCH):
                t.wait_ge(p.sem("sqb1"), c + 1)
                for f0, f1 in CB:
                    ins = t.matmul(ps_a[:, f0:f1], s_onesM[:], s_xb[:, f0:f1],
                                   start=(c == 0), stop=(c == NCH - 1))
                p.inc(ins, "smm1", 1)

        @blk.vector
        def _(v):
            v.wait_ge(p.sem("smm1"), NCH)
            p.chain(v, v.tensor_scalar(s_rb[:], ps_a[:, 0:TE], 1.0 / D, None,
                                       ALU.mult))
            p.chain(v, v.tensor_tensor(s_t2[:], s_mb[:], s_mb[:], ALU.mult))
            p.chain(v, v.tensor_tensor(s_rb[:], s_rb[:], s_t2[:],
                                       ALU.subtract))
            p.inc(v.tensor_scalar(s_rb[:], s_rb[:], EPS, None, ALU.add),
                  "var1", 1)

        @blk.scalar
        def _(a):
            p.wait(a, "var1")
            p.inc(a.activation(s_rb[:], s_rb[:], AF.Ln), "lnv1", 1)
            a.wait_ge(p.sem("lnv1"), 1)
            p.inc(a.activation(s_rb[:], s_rb[:], AF.Exp, scale=-0.5),
                  "rstd1", 1)

        loads_x(2, "rstd1", 1, "xu2")

        @blk.vector
        def _(v):
            v.wait_ge(p.sem("rstd1"), 1)
            for c in range(NCH):
                v.wait_ge(p.sem(f"xa2{c % 2}"), 16 * (c // 2 + 1))
                p.chain(v, v.tensor_tensor(s_t1[:], s_x[c % 2][:], s_mb[:],
                                           ALU.subtract))
                p.inc(v.tensor_tensor(s_xln[:, c * TE:(c + 1) * TE], s_t1[:],
                                      s_rb[:], ALU.mult), "xu2", 1)
                p.wait(v, "xu2")

        @blk.tensor
        def _(t):
            t.wait_ge(p.sem("ldc"), 16 * 9)
            t.wait_ge(p.sem("var1"), 1)
            for qq in range(4):
                ins = t.matmul(ps_a[:, qq * 128:(qq + 1) * 128], s_1r[:],
                               s_bvr[0:1, qq * 128:(qq + 1) * 128],
                               start=True, stop=True)
                p.inc(ins, "vbmm", 1)
                p.wait(t, "vbmm")

        @blk.vector
        def _(v):
            v.wait_ge(p.sem("vbmm"), 4)
            p.inc(v.tensor_copy(s_vbb[:], ps_a[:, 0:512]), "vbb", 1)
            p.wait(v, "vbb")

        @blk.gpsimd
        def _(g):
            j = 0
            for pg in range(8):
                for c in range(NCH):
                    if j >= 2:
                        g.wait_ge(p.sem("qcons"), j - 1)
                    p.inc(g.dma_start(out=s_w[j % 2][:, 0:256],
                                      in_=wq[c * 128:(c + 1) * 128,
                                             pg * 256:(pg + 1) * 256]),
                          f"ldwq{j % 2}", 16)
                    j += 1

        @blk.tensor
        def _(t):
            t.wait_ge(p.sem("xu2"), NCH)
            t.wait_ge(p.sem("ldc"), 16 * 9)
            t.wait_ge(p.sem("ldrope"), 16 * 4)
            j = 0
            for pg in range(8):
                if pg > 0:
                    t.wait_ge(p.sem("qev"), 2 * pg)
                for c in range(NCH):
                    t.wait_ge(p.sem(f"ldwq{j % 2}"), 16 * (j // 2 + 1))
                    for half in range(2):
                        for fi in range(2):
                            ins = t.matmul(
                                Q[half * 2 + fi],
                                s_w[j % 2][:, half * 128:(half + 1) * 128],
                                s_xln[:, c * TE + 512 + fi * 512:
                                      c * TE + 512 + fi * 512 + 512],
                                start=(c == 0), stop=(c == NCH - 1))
                    p.inc(ins, "qcons", 1)
                    j += 1
                for half in range(2):
                    dc = pg * 2 + half
                    t.wait_ge(p.sem("qev"), dc + 1)
                    if dc > 0:
                        t.wait_ge(p.sem("qrope"), dc)
                    for fi in range(2):
                        ins = t.matmul(ps_a[:, 1024 + fi * 512:
                                            1536 + fi * 512],
                                       s_rotm[:],
                                       s_q[:, dc * 1024 + fi * 512:
                                           dc * 1024 + fi * 512 + 512],
                                       start=True, stop=True)
                    p.inc(ins, "qrmm", 1)

        @blk.vector
        def _(v):
            for dc in range(NCH):
                v.wait_ge(p.sem("qcons"), (dc // 2 + 1) * NCH)
                for fi in range(2):
                    ins = v.tensor_scalar(
                        s_q[:, dc * 1024 + fi * 512:dc * 1024 + fi * 512 + 512],
                        Q[(dc % 2) * 2 + fi], s_bq[:, dc:dc + 1], None,
                        ALU.add)
                    p.inc(ins, "qev", 1) if fi == 1 else p.chain(v, ins)
                p.wait(v, "qev")
                v.wait_ge(p.sem("qrmm"), dc + 1)
                p.chain(v, v.tensor_tensor(s_t1[:, 0:1024],
                                           s_q[:, dc * 1024:(dc + 1) * 1024],
                                           s_cqb[:], ALU.mult))
                p.chain(v, v.tensor_tensor(s_t2[:, 0:1024], ps_a[:, 1024:2048],
                                           s_sqx[:], ALU.mult))
                p.inc(v.tensor_tensor(s_q[:, dc * 1024:(dc + 1) * 1024],
                                      s_t1[:, 0:1024], s_t2[:, 0:1024],
                                      ALU.add), "qrope", 1)
                p.wait(v, "qrope")

        @blk.gpsimd
        def _(g):
            g.wait_ge(p.sem("qcons"), 8 * NCH)
            j = 0
            for kc in range(4):
                for c in range(NCH):
                    if j >= 2:
                        g.wait_ge(p.sem("kcons"), j - 1)
                    p.inc(g.dma_start(out=s_w[j % 2][:, 0:128],
                                      in_=wkv[c * 128:(c + 1) * 128,
                                              kc * 128:(kc + 1) * 128]),
                          f"ldwk{j % 2}", 16)
                    j += 1

        @blk.tensor
        def _(t):
            t.wait_ge(p.sem("qrope"), NCH)
            j = 0
            for kc in range(4):
                if kc > 0:
                    t.wait_ge(p.sem("krope"), kc)
                for c in range(NCH):
                    t.wait_ge(p.sem(f"ldwk{j % 2}"), 16 * (j // 2 + 1))
                    for f0, f1 in CB:
                        ins = t.matmul(ps_a[:, f0:f1], s_w[j % 2][:, 0:128],
                                       s_xln[:, c * TE + f0:c * TE + f1],
                                       start=(c == 0), stop=(c == NCH - 1))
                    p.inc(ins, "kcons", 1)
                    j += 1
                t.wait_ge(p.sem("kact"), kc + 1)
                for f0, f1 in CB:
                    ins = t.matmul(ps_b[:, f0:f1], s_rotm[:], s_kb[:, f0:f1],
                                   start=True, stop=True)
                p.inc(ins, "krmm", 1)

        @blk.vector
        def _(v):
            for kc in range(4):
                v.wait_ge(p.sem("kcons"), (kc + 1) * NCH)
                p.chain(v, v.tensor_scalar(s_kb[:, 0:1024], ps_a[:, 0:1024],
                                           s_bkk[:, kc:kc + 1], None,
                                           ALU.add))
                p.inc(v.tensor_scalar(s_kb[:, 1024:TE], ps_a[:, 1024:TE],
                                      s_bkk[:, kc:kc + 1], None, ALU.add),
                      "kact", 1)
                p.wait(v, "kact")
                v.wait_ge(p.sem("krmm"), kc + 1)
                p.chain(v, v.tensor_tensor(s_t1[:], s_kb[:], s_ckb[:],
                                           ALU.mult))
                p.chain(v, v.tensor_tensor(s_t2[:], ps_b[:, 0:TE], s_skb[:],
                                           ALU.mult))
                p.inc(v.tensor_tensor(s_k[:, kc * TE:(kc + 1) * TE], s_t1[:],
                                      s_t2[:], ALU.add), "krope", 1)
                p.wait(v, "krope")

        @blk.gpsimd
        def _(g):
            g.wait_ge(p.sem("kcons"), 4 * NCH)
            j = 0
            for tt in range(13):
                for c in range(NCH):
                    if j >= 2:
                        g.wait_ge(p.sem("vcons"), j - 1)
                    p.inc(g.dma_start(out=s_w[j % 2][:, 0:512],
                                      in_=wkv[c * 128:(c + 1) * 128,
                                              512:1024]),
                          f"ldwv{j % 2}", 16)
                    j += 1

        @blk.tensor
        def _(t):
            t.wait_ge(p.sem("krope"), 4)
            j = 0
            for tt in range(13):
                if tt > 0:
                    t.wait_ge(p.sem("vev"), tt)
                for c in range(NCH):
                    t.wait_ge(p.sem(f"ldwv{j % 2}"), 16 * (j // 2 + 1))
                    ins = t.matmul(Q[2],
                                   s_xln[:, c * TE + tt * 128:
                                         c * TE + (tt + 1) * 128],
                                   s_w[j % 2][:, 0:512],
                                   start=(c == 0), stop=(c == NCH - 1))
                    p.inc(ins, "vcons", 1)
                    j += 1

        @blk.vector
        def _(v):
            v.wait_ge(p.sem("vbb"), 1)
            for tt in range(13):
                v.wait_ge(p.sem("vcons"), (tt + 1) * NCH)
                p.inc(v.tensor_tensor(s_v[:, tt * 512:(tt + 1) * 512], Q[2],
                                      s_vbb[:], ALU.add), "vev", 1)
                p.wait(v, "vev")

        def sc_thr(bi, i):
            if i >= 2:
                return 9 * bi + i - 1
            return 9 * bi if i == 0 else 9 * bi - 1

        @blk.tensor
        def _(t):
            t.wait_ge(p.sem("vev"), 13)
            for c2 in range(2):
                for h in range(NHEAD):
                    bi = c2 * NHEAD + h
                    kvh = h // 4
                    for i in range(9):
                        thr = sc_thr(bi, i)
                        if thr > 0:
                            t.wait_ge(p.sem("sexp"), thr)
                        if i < 8:
                            lk = s_k[:, kvh * TE + c2 * 512 + i * 128:
                                     kvh * TE + c2 * 512 + (i + 1) * 128]
                        else:
                            lk = s_k[:, kvh * TE + 1536:kvh * TE + 1664]
                        ins = t.matmul(Q[i % 2], lk,
                                       s_q[:, h * 1024 + c2 * 512:
                                           h * 1024 + (c2 + 1) * 512],
                                       start=True, stop=True)
                        p.inc(ins, "smm", 1)
                    t.wait_ge(p.sem("sexp"), 9 * (bi + 1))
                    for i in range(9):
                        vt = 4 * c2 + i if i < 8 else 12
                        ins = t.matmul(Q[2], s_onesM[:],
                                       s_E[:, i * 512:(i + 1) * 512],
                                       start=(i == 0), stop=(i == 8))
                        p.inc(ins, "rsmm", 1)
                        ins = t.matmul(Q[3],
                                       s_v[:, vt * 512 + kvh * 128:
                                           vt * 512 + (kvh + 1) * 128],
                                       s_E[:, i * 512:(i + 1) * 512],
                                       start=(i == 0), stop=(i == 8))
                        p.inc(ins, "avmm", 1)
                    t.wait_ge(p.sem("aoev"), bi + 1)

        @blk.vector
        def _(v):
            for c2 in range(2):
                for h in range(NHEAD):
                    bi = c2 * NHEAD + h
                    for i in range(9):
                        v.wait_ge(p.sem("smm"), 9 * bi + i + 1)
                        stt = s_st[:, (i % 2) * 512:(i % 2) * 512 + 512]
                        if i < 8:
                            msl = s_band[:, 896 - 128 * i:896 - 128 * i + 512]
                            p.inc(v.tensor_tensor(stt, Q[i % 2], msl, ALU.add),
                                  "smask", 1)
                            p.wait(v, "smask")
                        else:
                            p.chain(v, v.tensor_copy(stt, Q[i % 2]))
                            p.inc(v.tensor_tensor(
                                stt[0:16, 0:512], stt[0:16, 0:512],
                                s_meta[:, c2 * 512:(c2 + 1) * 512], ALU.add),
                                "smask", 1)
                            p.wait(v, "smask")
                    v.wait_ge(p.sem("rsmm"), 9 * (bi + 1))
                    p.chain(v, v.reciprocal(s_rec[:], Q[2]))
                    v.wait_ge(p.sem("avmm"), 9 * (bi + 1))
                    p.inc(v.tensor_tensor(s_ao[:, h * 1024 + c2 * 512:
                                               h * 1024 + (c2 + 1) * 512],
                                          Q[3], s_rec[:], ALU.mult),
                          "aoev", 1)
                    p.wait(v, "aoev")

        @blk.scalar
        def _(a):
            for c2 in range(2):
                for h in range(NHEAD):
                    bi = c2 * NHEAD + h
                    for i in range(9):
                        a.wait_ge(p.sem("smask"), 9 * bi + i + 1)
                        stt = s_st[:, (i % 2) * 512:(i % 2) * 512 + 512]
                        p.inc(a.activation(
                            s_E[:, i * 512:(i + 1) * 512], stt, AF.Exp,
                            bias=s_bkl[:, c2 * 9 + i:c2 * 9 + i + 1],
                            scale=ASCALE), "sexp", 1)

        @blk.gpsimd
        def _(g):
            g.wait_ge(p.sem("vcons"), 13 * NCH)
            j = 0
            for pg in range(8):
                for c in range(NCH):
                    if j >= 2:
                        g.wait_ge(p.sem("ocons"), j - 1)
                    p.inc(g.dma_start(out=s_w[j % 2][:, 0:256],
                                      in_=wo[c * 128:(c + 1) * 128,
                                             pg * 256:(pg + 1) * 256]),
                          f"ldwo{j % 2}", 16)
                    j += 1

        @blk.tensor
        def _(t):
            t.wait_ge(p.sem("aoev"), 32)
            j = 0
            for pg in range(8):
                if pg > 0:
                    t.wait_ge(p.sem("yrev"), 2 * pg)
                for c in range(NCH):
                    t.wait_ge(p.sem(f"ldwo{j % 2}"), 16 * (j // 2 + 1))
                    for half in range(2):
                        for fi in range(2):
                            ins = t.matmul(
                                Q[half * 2 + fi],
                                s_w[j % 2][:, half * 128:(half + 1) * 128],
                                s_ao[:, c * 1024 + fi * 512:
                                      c * 1024 + fi * 512 + 512],
                                start=(c == 0), stop=(c == NCH - 1))
                    p.inc(ins, "ocons", 1)
                    j += 1

        @blk.sync
        def _(sync):
            sync.wait_ge(p.sem("xu2"), NCH)
            for hid in range(NCH):
                if hid >= 2:
                    sync.wait_ge(p.sem(f"styo{hid % 2}"), 16 * (hid // 2))
                p.inc(sync.dma_start(out=s_x[hid % 2][:, 0:TC],
                                     in_=xTe[hid * 128:(hid + 1) * 128,
                                             512:512 + TC]),
                      f"ldxo{hid % 2}", 16)
                sync.wait_ge(p.sem("yrev"), hid + 1)
                p.inc(sync.dma_start(out=yT[hid * 128:(hid + 1) * 128, :],
                                     in_=s_x[hid % 2][:, 0:TC]),
                      f"styo{hid % 2}", 16)
            sync.wait_ge(p.sem("styo0"), 16 * 8)
            sync.wait_ge(p.sem("styo1"), 16 * 8)

        @blk.vector
        def _(v):
            for pg in range(8):
                v.wait_ge(p.sem("ocons"), (pg + 1) * NCH)
                for half in range(2):
                    hid = pg * 2 + half
                    v.wait_ge(p.sem(f"ldxo{hid % 2}"), 16 * (hid // 2 + 1))
                    for fi in range(2):
                        ins = v.scalar_tensor_tensor(
                            s_x[hid % 2][:, fi * 512:fi * 512 + 512],
                            Q[half * 2 + fi],
                            s_bo[:, hid:hid + 1],
                            s_x[hid % 2][:, fi * 512:fi * 512 + 512],
                            ALU.add, ALU.add)
                    p.inc(ins, "yrev", 1)
                    p.wait(v, "yrev")

    return nc


def _rope_tables(positions):
    inv = (10000.0 ** (-np.arange(0, 128, 2, dtype=np.float32) / 128.0)
           ).astype(np.float32)
    ang = positions.astype(np.float32)[None, :] * inv[:, None]
    cos = np.repeat(np.cos(ang), 2, axis=0).astype(np.float32)
    sin = np.repeat(np.sin(ang), 2, axis=0).astype(np.float32)
    return np.ascontiguousarray(cos), np.ascontiguousarray(sin)


def attn_host_inputs(me, xfull, Wq, bq, Wkv, bkv, Wo, bo):
    own0 = me * TC
    xe = np.zeros((TE, D), np.float32)
    if me > 0:
        xe[0:512] = xfull[own0 - 512:own0]
    xe[512:1536] = xfull[own0:own0 + TC]
    xe[1536:1552] = xfull[0:16]
    pq = np.arange(own0, own0 + TC)
    pe = np.zeros(TE, np.int64)
    pe[0:512] = np.maximum(own0 - 512 + np.arange(512), 0)
    pe[512:1536] = pq
    pe[1536:1552] = np.arange(16)
    cq, sq = _rope_tables(pq)
    ck, sk = _rope_tables(pe)
    band = np.full((128, 1408), -1e30, np.float32)
    for pp in range(128):
        band[pp, pp + 384:pp + 896] = 0.0
    meta = np.zeros((16, 1024), np.float32)
    for mk in range(16):
        meta[mk] = np.where(mk <= pq, 0.0, -1e30).astype(np.float32)
    bkl = np.zeros((128, 18), np.float32)
    for c2 in range(2):
        for i in range(8):
            kp = c2 * 512 + i * 128 + np.arange(128)
            bkl[:, c2 * 9 + i] = np.where(kp < 528 - 1024 * me, -1e30, 0.0)
        bkl[16:, c2 * 9 + 8] = -1e30
    rot = np.zeros((128, 128), np.float32)
    for i2 in range(64):
        rot[2 * i2 + 1, 2 * i2] = -1.0
        rot[2 * i2, 2 * i2 + 1] = 1.0
    return dict(
        xTe=np.ascontiguousarray(xe.T),
        wq=np.ascontiguousarray(Wq), wkv=np.ascontiguousarray(Wkv),
        wo=np.ascontiguousarray(Wo),
        bqc=np.ascontiguousarray(bq.reshape(NCH, 128).T),
        bkc=np.ascontiguousarray(bkv[0:512].reshape(4, 128).T),
        bvr=np.ascontiguousarray(bkv[512:1024].reshape(1, 512)
                                 ).astype(ml_dtypes.bfloat16),
        boc=np.ascontiguousarray(bo.reshape(NCH, 128).T),
        cosq=cq, sinq=sq, cosk=ck, sink=sk,
        bandm=band, metam=meta, bkill=bkl,
        rotm=rot.astype(ml_dtypes.bfloat16),
        ones1=np.ones((1, 128), ml_dtypes.bfloat16),
    )
